# version 46
# baseline (speedup 1.0000x reference)
import contextlib
import numpy as np
import concourse.bass as bass
import concourse.mybir as mybir
from concourse.bass_utils import run_bass_kernel_spmd

F32 = mybir.dt.float32
BF16 = mybir.dt.bfloat16
AF = mybir.ActivationFunctionType
ALU = mybir.AluOpType
AX = mybir.AxisListType

C_INT = 1.5 * 2**23
C_05 = 1.5 * 2**22
NEG_BIG = -1.0e9
EXP_SHIFT = -20.0
LN_EPS = 1e-5

SECTION = {"pe": "tensor", "dve": "vector", "act": "scalar", "pool": "gpsimd", "sp": "sync"}
CHAINED = ("dve", "act", "pool")


class Tok:
    __slots__ = ("sem", "val")
    def __init__(self, sem, val):
        self.sem = sem
        self.val = val


class Prog:

    def __init__(self, nc, sems, state):
        self.nc = nc
        self.sems = sems
        self.state = state
        self.ops = {e: [] for e in SECTION}

    @classmethod
    def fresh_state(cls, sems):
        return {
            "count": {name: 0 for name in sems},
            "hw": {e: {} for e in SECTION},
            "last": {e: None for e in SECTION},
            "barrier": {e: [] for e in SECTION},
        }

    def _tok(self, semname, amt):
        self.state["count"][semname] += amt
        return Tok(semname, self.state["count"][semname])

    def add(self, eng, method, *args, deps=(), inc=True, **kw):
        flat = []
        for d in deps:
            if d is None:
                continue
            if isinstance(d, list):
                flat.extend(x for x in d if x is not None)
            else:
                flat.append(d)
        deps = flat
        if self.state["barrier"][eng]:
            deps = self.state["barrier"][eng] + deps
            self.state["barrier"][eng] = []
        is_dma = method == "dma_start"
        dsem = kw.pop("dsem", None)
        if eng in CHAINED and not is_dma:
            if self.state["last"][eng] is not None:
                deps.append(self.state["last"][eng])
        if is_dma:
            assert dsem is not None, "dma_start needs dsem="
            tok = self._tok(dsem, 16)
            incspec = (dsem, 16)
        elif inc:
            tok = self._tok(eng, 1)
            incspec = (eng, 1)
        else:
            tok = None
            incspec = None
        self.ops[eng].append((method, args, kw, deps, incspec))
        if eng in CHAINED and not is_dma and tok is not None:
            self.state["last"][eng] = tok
        return tok

    def wait_only(self, eng, deps):
        self.ops[eng].append((None, (), {}, list(deps), None))

    def barrier(self):
        toks = [Tok(name, cnt) for name, cnt in self.state["count"].items() if cnt > 0]
        for e in SECTION:
            self.state["barrier"][e] = list(toks)

    def emit(self, blk):
        for ename, ops in self.ops.items():
            if not ops:
                continue
            def make(ops=ops, ename=ename):
                def body(engine):
                    hw = self.state["hw"][ename]
                    for (method, args, kw, deps, incspec) in ops:
                        for d in deps:
                            if hw.get(d.sem, -1) < d.val:
                                engine.wait_ge(self.sems[d.sem], d.val)
                                hw[d.sem] = d.val
                        if method is None:
                            continue
                        instr = getattr(engine, method)(*args, **kw)
                        if incspec is not None:
                            instr.then_inc(self.sems[incspec[0]], incspec[1])
                return body
            getattr(blk, SECTION[ename])(make())



def emit_ln(p, x_ap, out_ap, nch, W, st, deps, war_out=None):
    tacc = None
    for c in range(nch):
        p.add("act", "activation", st["sq"][:], x_ap[:, c, :], AF.Copy,
              accum_out=st["s1"][:, c:c + 1], deps=deps)
        tacc = p.add("act", "activation", st["sq"][:], x_ap[:, c, :], AF.Square,
                     accum_out=st["s2"][:, c:c + 1])
    s1, s2, m, v, r, nm, tmp = (st["s1"], st["s2"], st["m"], st["v"], st["r"],
                                st["nm"], st["tmp"])
    cc = slice(0, nch)
    p.add("dve", "tensor_scalar", m[:, cc], s1[:, cc], 1.0 / W, None, ALU.mult,
          deps=[tacc])
    p.add("dve", "tensor_scalar", v[:, cc], s2[:, cc], 1.0 / W, None, ALU.mult)
    p.add("dve", "tensor_tensor", tmp[:, cc], m[:, cc], m[:, cc], ALU.mult)
    p.add("dve", "tensor_sub", v[:, cc], v[:, cc], tmp[:, cc])
    p.add("dve", "tensor_scalar", v[:, cc], v[:, cc], LN_EPS, None, ALU.add)
    p.add("dve", "memset", r[:, cc], 1.0)
    for _ in range(5):
        p.add("dve", "tensor_tensor", tmp[:, cc], r[:, cc], r[:, cc], ALU.mult)
        p.add("dve", "tensor_tensor", tmp[:, cc], tmp[:, cc], v[:, cc], ALU.mult)
        p.add("dve", "tensor_scalar", tmp[:, cc], tmp[:, cc], -0.5, 1.5,
              ALU.mult, ALU.add)
        p.add("dve", "tensor_tensor", r[:, cc], r[:, cc], tmp[:, cc], ALU.mult)
    p.add("dve", "tensor_tensor", nm[:, cc], m[:, cc], r[:, cc], ALU.mult)
    trstd = p.add("dve", "tensor_scalar", nm[:, cc], nm[:, cc], -1.0, None, ALU.mult)
    toks = []
    for c in range(nch):
        dps = [trstd] + list(deps)
        if war_out is not None and c == 0:
            dps.append(war_out)
        toks.append(p.add("act", "activation", out_ap[:, c, :], x_ap[:, c, :],
                          AF.Identity, bias=st["nm"][:, c:c + 1],
                          scale=st["r"][:, c:c + 1], deps=dps))
    return toks


def emit_ln_stats(p, x_ap, nch, W, st, cols, deps):
    tacc = None
    for c in range(nch):
        p.add("act", "activation", st["sq"][:], x_ap[:, c, :], AF.Copy,
              accum_out=st["s1"][:, cols][:, c:c + 1], deps=deps)
        tacc = p.add("act", "activation", st["sq"][:], x_ap[:, c, :], AF.Square,
                     accum_out=st["s2"][:, cols][:, c:c + 1])
    s1, s2, m, v, r, nm, tmp = (st["s1"][:, cols], st["s2"][:, cols],
                                st["m"][:, cols], st["v"][:, cols],
                                st["r"][:, cols], st["nm"][:, cols],
                                st["tmp"][:, cols])
    p.add("dve", "tensor_scalar", m, s1, 1.0 / W, None, ALU.mult, deps=[tacc])
    p.add("dve", "tensor_scalar", v, s2, 1.0 / W, None, ALU.mult)
    p.add("dve", "tensor_tensor", tmp, m, m, ALU.mult)
    p.add("dve", "tensor_sub", v, v, tmp)
    p.add("dve", "tensor_scalar", v, v, LN_EPS, None, ALU.add)
    p.add("dve", "reciprocal", r, v)
    for _ in range(4):
        p.add("dve", "tensor_tensor", tmp, r, r, ALU.mult)
        p.add("dve", "tensor_tensor", tmp, tmp, v, ALU.mult)
        p.add("dve", "tensor_scalar", tmp, tmp, -0.5, 1.5, ALU.mult, ALU.add)
        p.add("dve", "tensor_tensor", r, r, tmp, ALU.mult)
    p.add("dve", "tensor_tensor", nm, m, r, ALU.mult)
    return p.add("dve", "tensor_scalar", nm, nm, -1.0, None, ALU.mult)


def make_qstate(ctx, nc, sbfn, prefix, QW, nsets):
    NB = QW // 16
    sets = []
    for r in range(nsets):
        sets.append({
            "am": sbfn(ctx, f"{prefix}am{r}", [128, NB], F32),
            "scale": sbfn(ctx, f"{prefix}sc{r}", [128, NB], F32),
            "recip": sbfn(ctx, f"{prefix}rc{r}", [128, NB], F32),
            "sclb": sbfn(ctx, f"{prefix}sb{r}", [128, NB], BF16),
            "y": sbfn(ctx, f"{prefix}y{r}", [128, QW], F32),
            "w1": sbfn(ctx, f"{prefix}w1{r}", [128, QW], F32),
            "A": sbfn(ctx, f"{prefix}A{r}", [128, QW], BF16),
            "B": sbfn(ctx, f"{prefix}B{r}", [128, QW], BF16),
            "sg": sbfn(ctx, f"{prefix}sg{r}", [128, QW], BF16),
        })
    return {"rot": 0, "sets": sets, "war": [dict() for _ in range(nsets)]}


def emit_quant(p, src_ap, out_ap, F, q, deps, out_war=None):
    r = q["rot"] % len(q["sets"])
    q["rot"] += 1
    b = q["sets"][r]
    war = q["war"][r]
    nb = F // 16
    src3 = src_ap.rearrange("p (nb s) -> p nb s", s=16)
    tred = p.add("dve", "tensor_reduce", b["am"][:, 0:nb], src3, AX.X, ALU.max,
                 apply_absolute_value=True, deps=deps + [war.get("am", [])])
    tsc = p.add("dve", "tensor_scalar", b["scale"][:, 0:nb], b["am"][:, 0:nb],
                1.0 / 6.0, 1e-12, ALU.mult, ALU.max, deps=[tred])
    tsb = p.add("dve", "tensor_copy", b["sclb"][:, 0:nb], b["scale"][:, 0:nb],
                deps=[war.get("sclb", [])])
    trc = p.add("dve", "reciprocal", b["recip"][:, 0:nb], b["scale"][:, 0:nb],
                deps=[war.get("recip", [])])
    tabs = p.add("act", "activation", b["y"][:, 0:F], src_ap, AF.Abs,
                 deps=deps + [war.get("y", [])])
    tsg = p.add("act", "activation", b["sg"][:, 0:F], src_ap, AF.Sign,
                deps=deps + [war.get("sg", [])])
    y3 = b["y"][:, 0:F].rearrange("p (nb s) -> p nb s", s=16)
    rb = b["recip"][:, 0:nb].rearrange("p nb -> p nb ()").broadcast_to([128, nb, 16])
    ty = p.add("dve", "tensor_tensor", y3, y3, rb, ALU.mult, deps=[tabs, trc])
    yF = b["y"][:, 0:F]
    wF = b["w1"][:, 0:F]
    aF = b["A"][:, 0:F]
    bF = b["B"][:, 0:F]
    p.add("dve", "tensor_scalar", wF, yF, 2.0, C_05, ALU.min, ALU.add,
          deps=[war.get("w1", [])])
    p.add("dve", "tensor_scalar", aF, wF, C_05, None, ALU.subtract,
          deps=[war.get("A", [])])
    p.add("dve", "tensor_scalar", wF, yF, 2.0, 4.0, ALU.max, ALU.min)
    tB = p.add("dve", "tensor_scalar", bF, wF, C_INT, C_INT + 2.0, ALU.add,
               ALU.subtract, deps=[war.get("B", [])])
    p.add("dve", "tensor_tensor", aF, aF, bF, ALU.add)
    tG = p.add("dve", "tensor_scalar", bF, yF, 5.0, 2.0, ALU.is_gt, ALU.mult)
    tlast = p.add("dve", "tensor_tensor", aF, aF, bF, ALU.add)
    tsgm = p.add("dve", "tensor_tensor", aF, aF, b["sg"][:, 0:F], ALU.mult,
                 deps=[tsg])
    a3 = b["A"][:, 0:F].rearrange("p (nb s) -> p nb s", s=16)
    o3 = out_ap.rearrange("p (nb s) -> p nb s", s=16)
    scb3 = b["sclb"][:, 0:nb].rearrange("p nb -> p nb ()").broadcast_to([128, nb, 16])
    tout = p.add("dve", "tensor_tensor", o3, a3, scb3, ALU.mult,
                 deps=[tsb, out_war])
    war["am"] = [tsc]
    war["sclb"] = [tout]
    war["recip"] = [ty]
    war["y"] = [tG]
    war["w1"] = [tB]
    war["sg"] = [tsgm]
    war["A"] = [tout]
    war["B"] = [tlast]
    return tout, [tred, tabs, tsg]



def build_program(S=8192, E=1024, I=2816):
    EC = E // 128
    NCH = S // 128
    NSLOT = NCH // 2
    NT = S // 256
    G = min(8, NSLOT)
    IC = I // 128
    IH = I // 2
    assert NSLOT % G == 0 and E % 128 == 0 and I % 256 == 0 and S % 1024 == 0
    NG = NSLOT // G
    HALFD = IC // 2
    GUH = (IC + 1) // 2
    iblocks = []
    off = 0
    while off < I:
        w = min(512, I - off)
        iblocks.append((off, w))
        off += w
    NIB = len(iblocks)
    NBMAX = max(E, IH, 512) // 16
    QW = max(E, IH, 512)

    nc = bass.Bass("TRN2", target_bir_lowering=False)

    xp = nc.dram_tensor("xp", [S, E], F32, kind="ExternalInput").ap()
    wqtd = nc.dram_tensor("wqt", [E, E], F32, kind="ExternalInput").ap()
    wktd = nc.dram_tensor("wkt", [E, E], F32, kind="ExternalInput").ap()
    wvtd = nc.dram_tensor("wvt", [E, E], F32, kind="ExternalInput").ap()
    bqd = nc.dram_tensor("bq", [128, EC], F32, kind="ExternalInput").ap()
    bkd = nc.dram_tensor("bk", [128, EC], F32, kind="ExternalInput").ap()
    bvd = nc.dram_tensor("bv", [1, E], F32, kind="ExternalInput").ap()
    wgd = nc.dram_tensor("wg", [I, E], F32, kind="ExternalInput").ap()
    wud = nc.dram_tensor("wu", [I, E], F32, kind="ExternalInput").ap()
    wdd = nc.dram_tensor("wd", [E, I], F32, kind="ExternalInput").ap()
    mskd = nc.dram_tensor("msk", [2, 128, 128], F32, kind="ExternalInput").ap()
    yd = nc.dram_tensor("y", [NSLOT, 128, E], F32, kind="ExternalOutput").ap()
    kfm = nc.dram_tensor("kfm", [EC, 128, S], BF16, kind="Internal").ap()
    vtm = nc.dram_tensor("vtm", [NCH, 128, E], BF16, kind="Internal").ap()
    qfm = nc.dram_tensor("qfm", [EC, 128, 128 * NSLOT], BF16, kind="Internal").ap()
    x1d = nc.dram_tensor("x1d", [NSLOT, 128, E], F32, kind="Internal").ap()
    wgtd = nc.dram_tensor("wgtd", [EC, 128, I], BF16, kind="Internal").ap()
    wutd = nc.dram_tensor("wutd", [EC, 128, I], BF16, kind="Internal").ap()
    wdtd = nc.dram_tensor("wdtd", [IC, 128, E], BF16, kind="Internal").ap()

    outer = contextlib.ExitStack()
    with outer:
        sems = {}
        for name in ("pe", "dve", "act", "pool", "d_init",
                     "xa0", "xa1", "xa2", "wl0", "wl1", "kst0", "kst1", "kk2",
                     "vst0", "vst1", "qst", "wfl",
                     "kk0", "kk1", "vv0", "vv1", "qgl", "xrl", "xs0", "xs1",
                     "wcl", "xl0", "xl1", "xl2", "ys0", "ys1"):
            sems[name] = outer.enter_context(nc.semaphore(name))
        state = Prog.fresh_state(sems)

        def sb(ctx, name, shape, dt):
            return ctx.enter_context(nc.sbuf_tensor(name, shape, dt))

        def psr(ctx, name, shape, dt):
            return ctx.enter_context(nc.psum_tensor(name, shape, dt))

        iden = sb(outer, "iden", [128, 128], BF16)
        cneg = sb(outer, "cneg", [128, 1], F32)
        ones_l = sb(outer, "ones_l", [128, 1], BF16)
        msk = sb(outer, "msk_sb", [128, 2, 128], F32)
        bq128 = sb(outer, "bq128", [128, EC], F32)
        bk128 = sb(outer, "bk128", [128, EC], F32)
        bq32 = sb(outer, "bq32", [128, EC], F32)

        with contextlib.ExitStack() as actx:
            p = Prog(nc, sems, state)
            xt = [sb(actx, f"xt{i}", [128, 2, E], F32) for i in range(3)]
            xln = [sb(actx, f"xln{i}", [128, 2, E], BF16) for i in range(2)]
            xfm = [sb(actx, f"xfm{i}", [128, EC, 256], BF16) for i in range(2)]
            ktb = [sb(actx, f"ktb{i}", [128, EC, 256], BF16) for i in range(2)]
            vtb = [sb(actx, f"vtb{i}", [128, 2, E], BF16) for i in range(2)]
            qtb = sb(actx, "qtb", [128, EC, 512], BF16)
            wqkv = {nm: sb(actx, nm + "t_sb", [128, EC, E], BF16)
                    for nm in ("wq", "wk", "wv")}
            wld = [sb(actx, f"wld{i}", [128, E], F32) for i in range(2)]
            st = {k: sb(actx, "ln_" + k, [128, 6], F32)
                  for k in ("s1", "s2", "m", "v", "r", "nm", "tmp")}
            st["sq"] = sb(actx, "ln_sq", [128, E], BF16)
            bvf = sb(actx, "bvf", [128, E], F32)
            sbv = sb(actx, "sbv", [1, E], F32)
            sbvb = sb(actx, "sbvb", [1, E], BF16)
            ones1 = sb(actx, "ones1", [1, 128], BF16)

            ptr = [psr(actx, f"ptr{i}", [128, 1024], BF16) for i in range(2)]
            kps = [psr(actx, f"kps{i}", [128, 512], F32) for i in range(2)]
            qps = [psr(actx, f"qps{i}", [128, 512], F32) for i in range(2)]
            vps = [psr(actx, f"vps{i}", [128, 512], F32) for i in range(2)]

            warb = {"ptr": {0: None, 1: None}, "kps": {0: None, 1: None},
                    "qps": {0: None, 1: None}, "vps": {0: None, 1: None}}
            p.add("pool", "memset", iden[:], 0.0)
            tid = p.add("pool", "affine_select", out=iden[:], in_=iden[:],
                        compare_op=ALU.not_equal, fill=1.0, base=0,
                        pattern=[[-1, 128]], channel_multiplier=1)
            p.add("pool", "memset", cneg[:], EXP_SHIFT)
            p.add("pool", "memset", ones_l[:], 1.0)
            t_ones = p.add("pool", "memset", ones1[:], 1.0)
            p.add("sp", "dma_start", msk[:], mskd.rearrange("i k q -> k i q"),
                  dsem="d_init")
            p.add("sp", "dma_start", bq128[:], bqd, dsem="d_init")
            p.add("sp", "dma_start", bk128[:], bkd, dsem="d_init")
            t_init = p.add("sp", "dma_start", sbv[:], bvd, dsem="d_init")
            t_msk = t_bq = t_bk = t_bv = t_init
            p.add("dve", "tensor_scalar", bq32[:], bq128[:], 1.0 / 32.0, None,
                  ALU.mult, deps=[t_bq])
            tbvb = p.add("dve", "tensor_copy", sbvb[:], sbv[:], deps=[t_bv])
            tbvf = None
            for h in range(0, E, 512):
                w = min(512, E - h)
                pi = (h // 512) % 2
                tmm = p.add("pe", "matmul", vps[pi][:, 0:w], ones1[:],
                            sbvb[:, h:h + w], start=True, stop=True,
                            deps=[tbvb, t_ones, warb["vps"][pi]])
                tbvf = p.add("dve", "tensor_copy", bvf[:, h:h + w],
                             vps[pi][:, 0:w], deps=[tmm])
                warb["vps"][pi] = tbvf

            wcast = {}
            widx = 0
            for nm, dsrc in (("wq", wqtd), ("wk", wktd), ("wv", wvtd)):
                for ec in range(EC):
                    buf = wld[widx % 2]
                    war = wcast.get(widx - 2)
                    tl = p.add("sp", "dma_start", buf[:, 0:E],
                               dsrc.rearrange("(ec pp) f -> ec pp f", pp=128)[ec],
                               deps=[war] if war else [], dsem=f"wl{widx % 2}")
                    wcast[widx] = p.add("dve", "tensor_copy", wqkv[nm][:, ec, :],
                                        buf[:, 0:E], deps=[tl])
                    widx += 1
            t_wq = wcast[EC - 1]
            t_wk = wcast[2 * EC - 1]
            t_wv = wcast[3 * EC - 1]

            xpt = xp.rearrange("(t c pp) e -> t pp c e", c=2, pp=128)
            tt = {}

            def a_load(t):
                if t >= NT:
                    return
                tt[("ld", t)] = p.add("sp", "dma_start", xt[t % 3][:], xpt[t],
                                      deps=[tt.get(("apply", t - 3))],
                                      dsem=f"xa{t % 3}")

            def a_stats(t):
                if t >= NT:
                    return
                cols = slice((t % 3) * 2, (t % 3) * 2 + 2)
                tt[("st", t)] = emit_ln_stats(p, xt[t % 3][:], 2, E, st, cols,
                                              [tt[("ld", t)],
                                               tt.get(("apply", t - 3))])

            def a_apply(t):
                if t >= NT:
                    return
                toks = []
                for c in range(2):
                    col = (t % 3) * 2 + c
                    dps = [tt[("st", t)]]
                    if c == 0:
                        dps.append(tt.get(("trdone", t - 2)))
                    toks.append(p.add("act", "activation", xln[t % 2][:, c, :],
                                      xt[t % 3][:, c, :], AF.Identity,
                                      bias=st["nm"][:, col:col + 1],
                                      scale=st["r"][:, col:col + 1], deps=dps))
                tt[("applies", t)] = toks
                tt[("apply", t)] = toks[-1]

            a_load(0); a_stats(0)
            a_load(1); a_stats(1)
            a_apply(0)

            for t in range(NT):
                bi = t % 2
                a_load(t + 2)
                apply_toks = tt[("applies", t)]
                dcp = None
                for c in range(2):
                    tts = None
                    for ec in range(EC):
                        dps = ()
                        if ec == 0:
                            dps = [apply_toks[c], tid, warb["ptr"][c]]
                        tts = p.add("pe", "transpose",
                                    ptr[c][:, ec * 128:(ec + 1) * 128],
                                    xln[bi][:, c, ec * 128:(ec + 1) * 128],
                                    iden[:], deps=dps, inc=(ec == EC - 1))
                    dcp = p.add("dve", "tensor_copy",
                                xfm[bi][:, :, c * 128:(c + 1) * 128],
                                ptr[c][:, 0:EC * 128].rearrange(
                                    "p (ec qq) -> p ec qq", qq=128),
                                deps=[tts, tt.get(("projdone", t - 2))])
                    warb["ptr"][c] = dcp
                tt[("trdone", t)] = dcp
                txfm = dcp
                kev = None
                for fc in range(EC):
                    tmm = None
                    for ec in range(EC):
                        dps = ()
                        if ec == 0:
                            dps = [txfm, t_wk, warb["kps"][fc % 2]]
                        tmm = p.add("pe", "matmul", kps[fc % 2][:, 0:256],
                                    wqkv["wk"][:, ec, fc * 128:(fc + 1) * 128],
                                    xfm[bi][:, ec, :],
                                    start=(ec == 0), stop=(ec == EC - 1),
                                    deps=dps, inc=(ec == EC - 1))
                    kev = p.add("act", "activation", ktb[bi][:, fc, :],
                                kps[fc % 2][:, 0:256], AF.Identity,
                                bias=bk128[:, fc:fc + 1],
                                deps=[tmm, t_bk, tt.get(("kst", t - 2))])
                    warb["kps"][fc % 2] = kev
                tt[("kst", t)] = p.add(
                    "pool", "dma_start",
                    kfm.rearrange("ec pp k -> pp ec k")[:, :, 256 * t:256 * (t + 1)],
                    ktb[bi][:], deps=[kev], dsem=f"kst{bi}")
                qev = None
                for fc in range(EC):
                    tmm = None
                    for ec in range(EC):
                        dps = ()
                        if ec == 0:
                            dps = [txfm, t_wq, warb["qps"][fc % 2]]
                        tmm = p.add("pe", "matmul", qps[fc % 2][:, 0:128],
                                    wqkv["wq"][:, ec, fc * 128:(fc + 1) * 128],
                                    xfm[bi][:, ec, 0:128],
                                    start=(ec == 0), stop=(ec == EC - 1),
                                    deps=dps, inc=(ec == EC - 1))
                    qev = p.add("act", "activation",
                                qtb[:, fc, (t % 4) * 128:(t % 4 + 1) * 128],
                                qps[fc % 2][:, 0:128], AF.Identity,
                                bias=bq32[:, fc:fc + 1], scale=1.0 / 32.0,
                                deps=[tmm, tt.get(("qst", 4 * (t // 4) - 1))])
                    warb["qps"][fc % 2] = qev
                if t % 4 == 3 or t == NT - 1:
                    u = t // 4
                    ncols = (t % 4 + 1) * 128
                    tt[("qst", t)] = p.add(
                        "pool", "dma_start",
                        qfm.rearrange("ec pp k -> pp ec k")[:, :, 512 * u:512 * u + ncols],
                        qtb[:, :, 0:ncols], deps=[qev], dsem="qst")
                vev = None
                vpi = 0
                for c in range(2):
                    for h in range(0, E, 512):
                        w = min(512, E - h)
                        pi = vpi % 2
                        vpi += 1
                        tmm = None
                        for ec in range(EC):
                            dps = ()
                            if ec == 0:
                                dps = [txfm, t_wv, tbvf, warb["vps"][pi]]
                            tmm = p.add("pe", "matmul", vps[pi][:, 0:w],
                                        xfm[bi][:, ec, c * 128:(c + 1) * 128],
                                        wqkv["wv"][:, ec, h:h + w],
                                        start=(ec == 0), stop=(ec == EC - 1),
                                        deps=dps, inc=(ec == EC - 1))
                        vev = p.add("dve", "tensor_tensor", vtb[bi][:, c, h:h + w],
                                    vps[pi][:, 0:w], bvf[:, h:h + w], ALU.add,
                                    deps=[tmm, tbvf, tt.get(("vst", t - 2))])
                        warb["vps"][pi] = vev
                tt[("vst", t)] = p.add(
                    "pool", "dma_start",
                    vtm.rearrange("kc pp e -> pp kc e")[:, 2 * t:2 * t + 2, :],
                    vtb[bi][:], deps=[vev], dsem=f"vst{bi}")
                tt[("projdone", t)] = vev
                tt[("lastuse", t)] = vev
                a_stats(t + 2)
                a_apply(t + 1)

            blk = actx.enter_context(nc.Block())
            p.emit(blk)

        with contextlib.ExitStack() as bctx:
            p = Prog(nc, sems, state)
            p.barrier()
            qg = [sb(bctx, f"qg{i}", [128, EC, 128 * G], BF16)
                  for i in range(2)]
            kk = [sb(bctx, f"kk{i}", [128, EC, 512], BF16) for i in range(3)]
            vv = [sb(bctx, f"vv{i}", [128, 4, E], BF16) for i in range(2)]
            og = sb(bctx, "og", [128, G, E + 1], F32)
            pt_all = [sb(bctx, f"pta{i}", [128, 4, 512], BF16) for i in range(2)]
            xr = sb(bctx, "xr", [128, E], F32)
            x1t = [sb(bctx, f"x1t{i}", [128, E], F32) for i in range(2)]
            lsb = sb(bctx, "lsb", [128, G], F32)
            linv = sb(bctx, "linv", [128, G], F32)
            wldb = [sb(bctx, f"wldb{i}", [128, max(E, IH)], F32)
                    for i in range(2)]
            wqbb = [sb(bctx, f"wqbb{i}", [128, max(E, IH)], BF16)
                    for i in range(2)]
            wstage = sb(bctx, "wstage", [128, E * IC // 2], BF16)
            qa = make_qstate(bctx, nc, sb, "qa_", QW, 2)
            stq = [psr(bctx, f"stq{i}", [128, 512], F32) for i in range(2)]
            avp = [psr(bctx, f"avp{i}", [128, E], F32) for i in range(2)]
            lps = psr(bctx, "lps", [128, 512], F32)
            wptr = psr(bctx, "wptr", [128, 1024], BF16)

            bt = {}
            sgi = 0
            bct = 0
            sit = 0
            itlast = {}
            pend_prev = []

            def emit_tail(tc):
                s_it, ar, kbi, nkc, kt, jj, g_, pb, boff, texp, tvl = (
                    tc["sit"], tc["ar"], tc["kbi"], tc["nkc"], tc["kt"],
                    tc["jj"], tc["g"], tc["pb"], tc["boff"], tc["texp"], tc["tvl"])
                tav = None
                for h in range(0, E, 512):
                    w = min(512, E - h)
                    for kc in range(nkc):
                        dps = ()
                        if kc == 0 and h == 0:
                            dps = [texp, tvl, bt.get(("og", s_it - 2))]
                        tav = p.add("pe", "matmul", avp[ar][:, h:h + w],
                                    pt_all[pb][:, kc, boff:boff + 128],
                                    vv[kbi][:, kc, h:h + w],
                                    start=(kc == 0), stop=(kc == nkc - 1),
                                    skip_group_check=True,
                                    deps=dps, inc=False)
                lcol = s_it
                assert lcol < 512
                tl_ = None
                for kc in range(nkc):
                    tl_ = p.add("pe", "matmul", lps[:, lcol:lcol + 1],
                                pt_all[pb][:, kc, boff:boff + 128],
                                ones_l[:],
                                start=(kc == 0), stop=(kc == nkc - 1),
                                skip_group_check=True,
                                inc=(kc == nkc - 1))
                bt[("l", s_it)] = tl_
                ogkey = ("ogslot", g_, jj)
                if kt == 0:
                    tog = p.add("dve", "tensor_copy", og[:, jj, 0:E],
                                avp[ar][:, 0:E],
                                deps=[tl_, bt.get(("fin", g_ - 1, jj))])
                    tog = p.add("dve", "tensor_copy", og[:, jj, E:E + 1],
                                lps[:, lcol:lcol + 1])
                else:
                    tog = p.add("dve", "tensor_add", og[:, jj, 0:E],
                                og[:, jj, 0:E], avp[ar][:, 0:E],
                                deps=[tl_, bt.get(ogkey)])
                    tog = p.add("dve", "tensor_add", og[:, jj, E:E + 1],
                                og[:, jj, E:E + 1], lps[:, lcol:lcol + 1])
                bt[ogkey] = tog
                bt[("og", s_it)] = tog

            wjobs = ([("wg", None, ti) for ti in range(IC)] +
                     [("wu", None, ti) for ti in range(IC)] +
                     [("wd", h, ti) for h in range(2) for ti in range(EC)])
            nwj = len(wjobs)
            wj = {"qtok": {}, "srel": {}, "wqb_war": {}, "stage_war": None,
                  "ptr_war": None}

            def wjob_head(idx):
                nm, h, ti = wjobs[idx]
                bi = idx % 2
                if nm == "wd":
                    F_ = IH
                    src_ = wdd.rearrange("(t pp) i -> t pp i", pp=128)[ti][:, IH * h:IH * (h + 1)]
                else:
                    F_ = E
                    dsrc = wgd if nm == "wg" else wud
                    src_ = dsrc.rearrange("(t pp) f -> t pp f", pp=128)[ti]
                tl = p.add("act", "dma_start", wldb[bi][:, 0:F_], src_,
                           deps=[wj["srel"].get(idx - 2, [])], dsem=f"wl{bi}")
                tq, srel = emit_quant(p, wldb[bi][:, 0:F_], wqbb[bi][:, 0:F_],
                                      F_, qa, [tl],
                                      out_war=wj["wqb_war"].get(bi))
                wj["qtok"][idx] = tq
                wj["srel"][idx] = srel

            def wjob_tail(idx):
                nm, h, ti = wjobs[idx]
                bi = idx % 2
                F_ = IH if nm == "wd" else E
                tq = wj["qtok"][idx]
                nblk = F_ // 128
                base = 0
                last_cp = None
                while base < nblk:
                    cnt = min(8, nblk - base)
                    tts = None
                    for b_ in range(cnt):
                        tts = p.add("pe", "transpose",
                                    wptr[:, b_ * 128:(b_ + 1) * 128],
                                    wqbb[bi][:, (base + b_) * 128:(base + b_ + 1) * 128],
                                    iden[:],
                                    deps=([tq, wj["ptr_war"]]
                                          if b_ == 0 else ()),
                                    inc=(b_ == cnt - 1))
                    srcv = wptr[:, 0:cnt * 128].rearrange(
                        "p (b_ qq) -> p b_ qq", qq=128)
                    if nm == "wd":
                        dst = wstage[:].rearrange("p (ic e) -> p ic e", e=E)[
                            :, base:base + cnt, ti * 128:(ti + 1) * 128]
                    else:
                        loc = ti % GUH
                        dst = wstage[:].rearrange("p (ec i) -> p ec i", ec=EC)[
                            :, :, loc * 128:(loc + 1) * 128]
                    stage_deps = [tts]
                    if wj["stage_war"] is not None:
                        stage_deps.append(wj["stage_war"])
                    last_cp = p.add("dve", "tensor_copy", dst, srcv,
                                    deps=stage_deps)
                    wj["ptr_war"] = last_cp
                    base += cnt
                wj["wqb_war"][bi] = last_cp
                if nm == "wd":
                    if ti == EC - 1:
                        dst = wdtd.rearrange("ic pp e -> pp ic e")[
                            :, h * HALFD:(h + 1) * HALFD, :]
                        srcd = wstage[:].rearrange("p (ic e) -> p ic e", e=E)[
                            :, 0:HALFD, :]
                        wj["stage_war"] = p.add("pool", "dma_start", dst, srcd,
                                                deps=[last_cp], dsem="wfl")
                else:
                    loc = ti % GUH
                    if loc == GUH - 1 or ti == IC - 1:
                        hidx = ti // GUH
                        base_i = hidx * GUH * 128
                        ncols = (loc + 1) * 128
                        ddst = wgtd if nm == "wg" else wutd
                        dst = ddst.rearrange("ec pp i -> pp ec i")[
                            :, :, base_i:base_i + ncols]
                        srcd = wstage[:].rearrange("p (ec i) -> p ec i", ec=EC)[
                            :, :, 0:ncols]
                        wj["stage_war"] = p.add("pool", "dma_start", dst, srcd,
                                                deps=[last_cp], dsem="wfl")

            NKT_TOT = sum((256 * (gg * G + G) + 511) // 512 for gg in range(NG))
            wji = 0
            gkt = 0

            for g in range(NG):
                s0 = g * G
                tqg = p.add("sp", "dma_start", qg[g % 2][:],
                            qfm.rearrange("ec pp k -> pp ec k")[:, :, 128 * s0:128 * (s0 + G)],
                            deps=[bt.get(("qcons", g - 2))], dsem="qgl")
                nkt = (256 * (s0 + G) + 511) // 512
                for kt in range(nkt):
                    kbi = kt % 2
                    kki = kt % 3
                    prev2 = itlast.get((g, kt - 2), itlast.get((g - 1, "last")))
                    kwar = bt.get(("l", prev2)) if prev2 is not None else None
                    prev3 = itlast.get((g, kt - 3), itlast.get((g - 1, "last")))
                    kwar3 = bt.get(("l", prev3)) if prev3 is not None else None
                    tkl = p.add("sp", "dma_start", kk[kki][:],
                                kfm.rearrange("ec pp k -> pp ec k")[:, :, 512 * kt:512 * (kt + 1)],
                                deps=[kwar3], dsem=f"kk{kki}")
                    tvl = p.add("sp", "dma_start", vv[kbi][:],
                                vtm.rearrange("kc pp e -> pp kc e")[:, 4 * kt:4 * kt + 4, :],
                                deps=[kwar], dsem=f"vv{kbi}")
                    full, single = [], []
                    for jj in range(G):
                        keys = 256 * (s0 + jj + 1) - 512 * kt
                        if keys <= 0:
                            continue
                        (full if keys > 512 else single).append(jj)
                    batches = [full[i:i + 4] for i in range(0, len(full), 4)]
                    batches += [[jj] for jj in single]
                    for bslots in batches:
                        jj0, w = bslots[0], len(bslots)
                        assert bslots == list(range(jj0, jj0 + w))
                        keys0 = 256 * (s0 + bslots[-1] + 1) - 512 * kt
                        nkc = min(4, max(k_ // 128 for k_ in
                                         [256 * (s0 + j_ + 1) - 512 * kt
                                          for j_ in bslots]))
                        is_last = keys0 <= 512
                        pb = bct % 2
                        bct += 1
                        texp = None
                        for kc in range(nkc):
                            srot = sgi % 2
                            sgi += 1
                            tmm = None
                            for ec in range(EC):
                                dps = ()
                                if ec == 0:
                                    dps = [tkl, tqg, bt.get(("sexp", sgi - 1 - 2))]
                                tmm = p.add("pe", "matmul",
                                            stq[srot][:, 0:w * 128],
                                            kk[kki][:, ec, kc * 128:(kc + 1) * 128],
                                            qg[g % 2][:, ec, (jj0) * 128:(jj0 + w) * 128],
                                            start=(ec == 0), stop=(ec == EC - 1),
                                            deps=dps,
                                            inc=(ec == EC - 1))
                            bt[("sc", sgi - 1)] = tmm
                            tsc = tmm
                            if is_last and w == 1 and kc >= nkc - 2:
                                mi = kc - (nkc - 2)
                                tsc = p.add("dve", "tensor_tensor",
                                            stq[srot][:, 0:128],
                                            stq[srot][:, 0:128],
                                            msk[:, mi, :], ALU.add,
                                            deps=[tsc, t_msk])
                            texp = p.add("act", "activation",
                                         pt_all[pb][:, kc, 0:w * 128],
                                         stq[srot][:, 0:w * 128], AF.Exp,
                                         bias=cneg[:],
                                         deps=[tsc, bt.get(("btail", bct - 1 - 2))])
                            bt[("sexp", sgi - 1)] = texp
                        newtails = []
                        for bi_, jj in enumerate(bslots):
                            keys = 256 * (s0 + jj + 1) - 512 * kt
                            newtails.append(dict(
                                sit=sit, ar=sit % 2, kbi=kbi,
                                nkc=min(4, keys // 128), kt=kt, jj=jj, g=g,
                                pb=pb, boff=bi_ * 128, texp=texp, tvl=tvl))
                            sit += 1
                        for tc in pend_prev:
                            emit_tail(tc)
                        if pend_prev:
                            bt[("btail", pend_prev[0]["bct"])] = bt[("l", pend_prev[-1]["sit"])]
                        for tc in newtails:
                            tc["bct"] = bct - 1
                        pend_prev = newtails
                        nb_ = len(batches)
                        bix_ = batches.index(bslots) + 1
                        tgt = (gkt * nb_ + bix_) * nwj // (NKT_TOT * nb_)
                        while wji < nwj and wji <= tgt - 1:
                            if wji >= 2:
                                wjob_tail(wji - 2)
                            wjob_head(wji)
                            wji += 1
                    itlast[(g, kt)] = sit - 1
                    gkt += 1
                itlast[(g, "last")] = sit - 1
                for tc in pend_prev:
                    emit_tail(tc)
                if pend_prev:
                    bt[("btail", pend_prev[0]["bct"])] = bt[("l", pend_prev[-1]["sit"])]
                pend_prev = []
                bt[("qcons", g)] = bt[("sc", sgi - 1)]
                ogdone = [bt.get(("ogslot", g, jj)) for jj in range(G)]
                tlev = p.add("dve", "tensor_copy",
                             lsb[:].rearrange("p g -> p g ()"),
                             og[:, :, E:E + 1],
                             deps=[bt[("og", sit - 1)]] + ogdone)
                p.add("dve", "reciprocal", linv[:], lsb[:])
                for jj in range(G):
                    j = s0 + jj
                    txr = p.add("sp", "dma_start", xr[:],
                                xp.rearrange("(n pp) e -> n pp e", pp=128)[2 * j],
                                deps=[bt.get("lastfin")], dsem="xrl")
                    p.add("dve", "tensor_scalar", x1t[jj % 2][:],
                          og[:, jj, 0:E], linv[:, jj:jj + 1], None, ALU.mult,
                          deps=[bt.get(("x1war", jj % 2)),
                                bt.get(("ogslot", g, jj))])
                    tfin = p.add("dve", "tensor_add", x1t[jj % 2][:],
                                 x1t[jj % 2][:], xr[:], deps=[txr])
                    bt[("fin", g, jj)] = tfin
                    bt["lastfin"] = tfin
                    bt[("x1war", jj % 2)] = p.add(
                        "pool", "dma_start", x1d[j], x1t[jj % 2][:],
                        deps=[tfin], dsem=f"xs{jj % 2}")

            while wji < nwj:
                if wji >= 2:
                    wjob_tail(wji - 2)
                wjob_head(wji)
                wji += 1
            for j_ in range(max(0, nwj - 2), nwj):
                wjob_tail(j_)

            blk = bctx.enter_context(nc.Block())
            p.emit(blk)

        with contextlib.ExitStack() as cctx:
            p = Prog(nc, sems, state)
            p.barrier()
            wgt = sb(cctx, "wgt_sb", [128, EC, I], BF16)
            wut = sb(cctx, "wut_sb", [128, EC, I], BF16)
            wdt = sb(cctx, "wdt_sb", [128, IC, E], BF16)
            x1b = [sb(cctx, f"x1b{i}", [128, E], F32) for i in range(2)]
            xlnc = sb(cctx, "xlnc", [128, E], F32)
            xqb = sb(cctx, "xqb", [128, E], BF16)
            xqf = [sb(cctx, f"xqf{i}", [128, EC, 128], BF16) for i in range(2)]
            sil = [sb(cctx, f"sil{i}", [128, 512], F32) for i in range(2)]
            hbuf = [sb(cctx, f"hbuf{i}", [128, 1024], F32) for i in range(2)]
            hqb = sb(cctx, "hqb", [128, 1024], BF16)
            hqf = [sb(cctx, f"hqf{i}", [128, 8, 128], BF16) for i in range(2)]
            ytb = sb(cctx, "ytb", [128, E], F32)
            stc = {k: sb(cctx, "lnc_" + k, [128, 3], F32)
                   for k in ("s1", "s2", "m", "v", "r", "nm", "tmp")}
            stc["sq"] = sb(cctx, "lnc_sq", [128, E], BF16)
            qc = make_qstate(cctx, nc, sb, "qc_", max(E, 512), 2)
            ptc = [psr(cctx, "ptc0", [128, 1024], BF16)]
            gps = [psr(cctx, f"gps{i}", [128, 512], F32) for i in range(3)]
            ups = [psr(cctx, f"ups{i}", [128, 512], F32) for i in range(2)]
            dps_ = psr(cctx, "dps", [128, max(E, 512)], F32)

            p.add("sp", "dma_start", wgt[:],
                  wgtd.rearrange("ec pp i -> pp ec i"), dsem="wcl")
            p.add("sp", "dma_start", wut[:],
                  wutd.rearrange("ec pp i -> pp ec i"), dsem="wcl")
            twd = p.add("sp", "dma_start", wdt[:],
                        wdtd.rearrange("ic pp e -> pp ic e"), dsem="wcl")
            twg = twu = twd

            ct = {}
            trc = {"n": 0}

            def c_load(s):
                if s >= NSLOT:
                    return
                ct[("ld", s)] = p.add("sp", "dma_start", x1b[s % 2][:], x1d[s],
                                      deps=[ct.get(("res", s - 2))],
                                      dsem=f"xl{s % 3}")

            def c_stats(s):
                if s >= NSLOT:
                    return
                x1v = x1b[s % 2][:].rearrange("p (c e) -> p c e", c=1)
                ct[("st", s)] = emit_ln_stats(p, x1v, 1, E, stc,
                                              slice(s % 3, s % 3 + 1),
                                              [ct[("ld", s)],
                                               ct.get(("ap", s - 3))])

            def c_stage1(s):
                if s >= NSLOT:
                    return
                bi = s % 2
                tap = p.add("act", "activation", xlnc[:], x1b[s % 2][:],
                            AF.Identity, bias=stc["nm"][:, s % 3:s % 3 + 1],
                            scale=stc["r"][:, s % 3:s % 3 + 1],
                            deps=[ct[("st", s)], ct.get(("xqdone", s - 1))])
                ct[("ap", s)] = tap
                tq, srel = emit_quant(p, xlnc[:], xqb[:], E, qc,
                                      [tap], out_war=ct.get(("xqtr", s - 1)))
                ct[("xq", s)] = tq
                ct[("xqdone", s)] = srel

            def c_stage1_tr(s):
                if s >= NSLOT:
                    return
                bi = s % 2
                ri = 0
                tts = None
                for ec in range(EC):
                    dps = ()
                    if ec == 0:
                        dps = [ct[("xq", s)], tid, ct.get(("ptc", ri))]
                    tts = p.add("pe", "transpose",
                                ptc[ri][:, ec * 128:(ec + 1) * 128],
                                xqb[:, ec * 128:(ec + 1) * 128], iden[:],
                                deps=dps, inc=(ec == EC - 1))
                ct[("xqtr", s)] = tts
                txqf = p.add("act", "activation", xqf[bi][:].rearrange(
                                 "p ec qq -> p (ec qq)"),
                             ptc[ri][:, 0:EC * 128], AF.Copy,
                             deps=[tts, ct.get(("xqfcons", s - 2))])
                ct[("ptc", ri)] = txqf
                ct[("xqf", s)] = txqf

            c_load(0); c_stats(0)
            c_load(1); c_stats(1)
            c_stage1(0); c_stage1_tr(0)

            hgroups = []
            off = 0
            while off < I:
                w = min(1024, I - off)
                hgroups.append((off, w))
                off += w
            NHG = len(hgroups)

            def chunk_head(s, gib, hb, loc, ioff, cw):
                bi = s % 2
                bb = gib % 2
                hbr = hb % 2
                gp, up_ = gps[gib % 3], ups[bb]
                tg = None
                for ec in range(EC):
                    dps = ()
                    if ec == 0:
                        dps = [ct[("xqf", s)], twg, ct.get(("sil", gib - 3))]
                    tg = p.add("pe", "matmul", gp[:, 0:cw],
                               xqf[bi][:, ec, :],
                               wgt[:, ec, ioff:ioff + cw],
                               start=(ec == 0), stop=(ec == EC - 1),
                               deps=dps, inc=(ec == EC - 1))
                ct[("gate_last", s)] = tg
                tu = None
                for ec in range(EC):
                    dps = ()
                    if ec == 0:
                        dps = [twu, ct.get(("h", gib - 2))]
                    tu = p.add("pe", "matmul", up_[:, 0:cw],
                               xqf[bi][:, ec, :],
                               wut[:, ec, ioff:ioff + cw],
                               start=(ec == 0), stop=(ec == EC - 1),
                               deps=dps, inc=(ec == EC - 1))
                ct[("up_last", s)] = tu
                tsil = p.add("act", "activation", sil[bb][:, 0:cw],
                             gp[:, 0:cw], AF.Silu,
                             deps=[tg, ct.get(("h", gib - 2))])
                ct[("sil", gib)] = tsil
                th = p.add("dve", "tensor_tensor", hbuf[hbr][:, loc:loc + cw],
                           sil[bb][:, 0:cw], up_[:, 0:cw], ALU.mult,
                           deps=[tsil, tu, ct.get(("hrel", hb - 2))])
                ct[("h", gib)] = th
                return th

            def group_tail(hb):
                qoff, qw = hgroups[hb % NHG]
                hbr = hb % 2
                nkc = qw // 128
                ri = 0
                tts2 = None
                for kc in range(nkc):
                    dps = ()
                    if kc == 0:
                        dps = [ct[("hq", hb)], ct.get(("ptc", ri))]
                    tts2 = p.add("pe", "transpose",
                                 ptc[ri][:, kc * 128:(kc + 1) * 128],
                                 hqb[:, kc * 128:(kc + 1) * 128],
                                 iden[:], deps=dps, inc=(kc == nkc - 1))
                ct[("hqtr", hb)] = tts2
                thqf = p.add("act", "activation",
                             hqf[hbr][:, 0:nkc, :].rearrange(
                                 "p kc qq -> p (kc qq)"),
                             ptc[ri][:, 0:nkc * 128], AF.Copy,
                             deps=[tts2, ct.get(("hqfcons", hb - 2))])
                ct[("ptc", ri)] = thqf
                s = hb // NHG
                qi = hb % NHG
                td = None
                for kc in range(nkc):
                    icg = qoff // 128 + kc
                    for h in range(0, E, 512):
                        w = min(512, E - h)
                        dps = ()
                        if kc == 0 and h == 0:
                            dps = [thqf, twd]
                            if qi == 0:
                                dps.append(ct.get(("res", s - 1)))
                        td = p.add("pe", "matmul", dps_[:, h:h + w],
                                   hqf[hbr][:, kc, :],
                                   wdt[:, icg, h:h + w],
                                   start=(qi == 0 and kc == 0),
                                   stop=(qi == NHG - 1 and kc == nkc - 1),
                                   skip_group_check=True,
                                   deps=dps,
                                   inc=(kc == nkc - 1 and h + w >= E))
                ct[("hqfcons", hb)] = td
                if qi == NHG - 1:
                    ct[("down", s)] = td

            def residual(s):
                bi = s % 2
                tres = p.add("dve", "tensor_add", ytb[:], x1b[s % 2][:],
                             dps_[:, 0:E],
                             deps=[ct[("down", s)], ct.get(("yst", s - 1))])
                ct[("res", s)] = tres
                ct[("yst", s)] = p.add("pool", "dma_start", yd[s], ytb[:],
                                       deps=[tres], dsem=f"ys{bi}")

            gib = 0
            for s in range(NSLOT):
                for qi, (qoff, qw) in enumerate(hgroups):
                    hb = s * NHG + qi
                    hbr = hb % 2
                    ths = []
                    loc = 0
                    while loc < qw:
                        cw = min(512, qw - loc)
                        ths.append(chunk_head(s, gib, hb, loc, qoff + loc, cw))
                        gib += 1
                        loc += cw
                    if hb >= 1:
                        group_tail(hb - 1)
                    if qi == 2:
                        c_stage1_tr(s + 1)
                    thq, hrel = emit_quant(p, hbuf[hbr][:, 0:qw],
                                           hqb[:, 0:qw], qw, qc, ths,
                                           out_war=ct.get(("hqtr", hb - 1)))
                    ct[("hrel", hb)] = hrel
                    ct[("hq", hb)] = thq
                    if hb >= 1 and (hb - 1) % NHG == NHG - 1:
                        residual((hb - 1) // NHG)
                        c_load((hb - 1) // NHG + 2)
                        c_stats((hb - 1) // NHG + 2)
                    if qi == 1:
                        c_stage1(s + 1)
                ct[("xqfcons", s)] = ct[("up_last", s)]
            group_tail(NSLOT * NHG - 1)
            residual(NSLOT - 1)

            p.wait_only("pool", [Tok("ys0", p.state["count"]["ys0"]),
                                 Tok("ys1", p.state["count"]["ys1"])])
            blk = cctx.enter_context(nc.Block())
            p.emit(blk)

    nc.finalize()
    return nc



def make_core_inputs(x, Wq, bq, Wk, bk, Wv, bv, wg, wu, wd, S, E, I, core):
    n, par = core // 2, core % 2
    xv = np.ascontiguousarray(x[n], dtype=np.float32)
    if par == 1:
        xv = np.ascontiguousarray(
            xv.reshape(S // 256, 2, 128, E)[:, ::-1].reshape(S, E))
    tri = np.where(np.arange(128)[None, :] >= np.arange(128)[:, None],
                   np.float32(0.0), np.float32(NEG_BIG))
    other = np.full((128, 128), (NEG_BIG if par == 0 else 0.0), np.float32)
    msk = np.stack([tri, other]).astype(np.float32)
    return {
        "xp": xv,
        "wqt": np.ascontiguousarray(Wq.T, dtype=np.float32),
        "wkt": np.ascontiguousarray(Wk.T, dtype=np.float32),
        "wvt": np.ascontiguousarray(Wv.T, dtype=np.float32),
        "bq": np.ascontiguousarray(bq.reshape(E // 128, 128).T, dtype=np.float32),
        "bk": np.ascontiguousarray(bk.reshape(E // 128, 128).T, dtype=np.float32),
        "bv": np.ascontiguousarray(bv.reshape(1, E), dtype=np.float32),
        "wg": np.ascontiguousarray(wg, dtype=np.float32),
        "wu": np.ascontiguousarray(wu, dtype=np.float32),
        "wd": np.ascontiguousarray(wd, dtype=np.float32),
        "msk": msk,
    }


def assemble_output(results, S, E, B=4):
    out = np.empty((B, S, E), np.float32)
    for core in range(2 * B):
        n, par = core // 2, core % 2
        yv = np.asarray(results[core]["y"])
        out[n].reshape(S // 256, 2, 128, E)[:, par] = yv
    return out


def kernel(L0_attn_x, L0_attn_Wq, L0_attn_bq, L0_attn_Wk, L0_attn_bk,
           L0_attn_Wv, L0_attn_bv, L0_ffn_w_gate, L0_ffn_w_up, L0_ffn_w_down):
    S, E, I = 8192, 1024, 2816
    x = np.asarray(L0_attn_x, dtype=np.float32)
    Wq, bq, Wk, bk, Wv, bv, wg, wu, wd = [
        np.asarray(a, dtype=np.float32) for a in
        (L0_attn_Wq, L0_attn_bq, L0_attn_Wk, L0_attn_bk, L0_attn_Wv,
         L0_attn_bv, L0_ffn_w_gate, L0_ffn_w_up, L0_ffn_w_down)]
    nc = build_program(S, E, I)
    in_maps = [make_core_inputs(x, Wq, bq, Wk, bk, Wv, bv, wg, wu, wd,
                                S, E, I, core) for core in range(8)]
    res = run_bass_kernel_spmd(nc, in_maps, core_ids=list(range(8)))
    return assemble_output(res.results, S, E)


# version 47
# speedup vs baseline: 1.1793x; 1.1793x over previous
import contextlib
import numpy as np
import concourse.bass as bass
import concourse.mybir as mybir
from concourse.bass_utils import run_bass_kernel_spmd

F32 = mybir.dt.float32
BF16 = mybir.dt.bfloat16
AF = mybir.ActivationFunctionType
ALU = mybir.AluOpType
AX = mybir.AxisListType

C_INT = 1.5 * 2**23
C_05 = 1.5 * 2**22
NEG_BIG = -1.0e9
EXP_SHIFT = -20.0
LN_EPS = 1e-5

SECTION = {"pe": "tensor", "dve": "vector", "act": "scalar", "pool": "gpsimd", "sp": "sync"}
CHAINED = ("dve", "act", "pool")


class Tok:
    __slots__ = ("sem", "val")
    def __init__(self, sem, val):
        self.sem = sem
        self.val = val


class Prog:

    def __init__(self, nc, sems, state):
        self.nc = nc
        self.sems = sems
        self.state = state
        self.ops = {e: [] for e in SECTION}

    @classmethod
    def fresh_state(cls, sems):
        return {
            "count": {name: 0 for name in sems},
            "hw": {e: {} for e in SECTION},
            "last": {e: None for e in SECTION},
            "barrier": {e: [] for e in SECTION},
        }

    def _tok(self, semname, amt):
        self.state["count"][semname] += amt
        return Tok(semname, self.state["count"][semname])

    def add(self, eng, method, *args, deps=(), inc=True, **kw):
        flat = []
        for d in deps:
            if d is None:
                continue
            if isinstance(d, list):
                flat.extend(x for x in d if x is not None)
            else:
                flat.append(d)
        deps = flat
        if self.state["barrier"][eng]:
            deps = self.state["barrier"][eng] + deps
            self.state["barrier"][eng] = []
        is_dma = method == "dma_start"
        dsem = kw.pop("dsem", None)
        if eng in CHAINED and not is_dma:
            if self.state["last"][eng] is not None:
                deps.append(self.state["last"][eng])
        if is_dma:
            assert dsem is not None, "dma_start needs dsem="
            tok = self._tok(dsem, 16)
            incspec = (dsem, 16)
        elif inc:
            tok = self._tok(eng, 1)
            incspec = (eng, 1)
        else:
            tok = None
            incspec = None
        self.ops[eng].append((method, args, kw, deps, incspec))
        if eng in CHAINED and not is_dma and tok is not None:
            self.state["last"][eng] = tok
        return tok

    def wait_only(self, eng, deps):
        self.ops[eng].append((None, (), {}, list(deps), None))

    def barrier(self):
        toks = [Tok(name, cnt) for name, cnt in self.state["count"].items() if cnt > 0]
        for e in SECTION:
            self.state["barrier"][e] = list(toks)

    def emit(self, blk):
        for ename, ops in self.ops.items():
            if not ops:
                continue
            def make(ops=ops, ename=ename):
                def body(engine):
                    hw = self.state["hw"][ename]
                    for (method, args, kw, deps, incspec) in ops:
                        for d in deps:
                            if hw.get(d.sem, -1) < d.val:
                                engine.wait_ge(self.sems[d.sem], d.val)
                                hw[d.sem] = d.val
                        if method is None:
                            continue
                        instr = getattr(engine, method)(*args, **kw)
                        if incspec is not None:
                            instr.then_inc(self.sems[incspec[0]], incspec[1])
                return body
            getattr(blk, SECTION[ename])(make())



def emit_ln(p, x_ap, out_ap, nch, W, st, deps, war_out=None):
    tacc = None
    for c in range(nch):
        p.add("act", "activation", st["sq"][:], x_ap[:, c, :], AF.Copy,
              accum_out=st["s1"][:, c:c + 1], deps=deps)
        tacc = p.add("act", "activation", st["sq"][:], x_ap[:, c, :], AF.Square,
                     accum_out=st["s2"][:, c:c + 1])
    s1, s2, m, v, r, nm, tmp = (st["s1"], st["s2"], st["m"], st["v"], st["r"],
                                st["nm"], st["tmp"])
    cc = slice(0, nch)
    p.add("dve", "tensor_scalar", m[:, cc], s1[:, cc], 1.0 / W, None, ALU.mult,
          deps=[tacc])
    p.add("dve", "tensor_scalar", v[:, cc], s2[:, cc], 1.0 / W, None, ALU.mult)
    p.add("dve", "tensor_tensor", tmp[:, cc], m[:, cc], m[:, cc], ALU.mult)
    p.add("dve", "tensor_sub", v[:, cc], v[:, cc], tmp[:, cc])
    p.add("dve", "tensor_scalar", v[:, cc], v[:, cc], LN_EPS, None, ALU.add)
    p.add("dve", "memset", r[:, cc], 1.0)
    for _ in range(5):
        p.add("dve", "tensor_tensor", tmp[:, cc], r[:, cc], r[:, cc], ALU.mult)
        p.add("dve", "tensor_tensor", tmp[:, cc], tmp[:, cc], v[:, cc], ALU.mult)
        p.add("dve", "tensor_scalar", tmp[:, cc], tmp[:, cc], -0.5, 1.5,
              ALU.mult, ALU.add)
        p.add("dve", "tensor_tensor", r[:, cc], r[:, cc], tmp[:, cc], ALU.mult)
    p.add("dve", "tensor_tensor", nm[:, cc], m[:, cc], r[:, cc], ALU.mult)
    trstd = p.add("dve", "tensor_scalar", nm[:, cc], nm[:, cc], -1.0, None, ALU.mult)
    toks = []
    for c in range(nch):
        dps = [trstd] + list(deps)
        if war_out is not None and c == 0:
            dps.append(war_out)
        toks.append(p.add("act", "activation", out_ap[:, c, :], x_ap[:, c, :],
                          AF.Identity, bias=st["nm"][:, c:c + 1],
                          scale=st["r"][:, c:c + 1], deps=dps))
    return toks


def emit_ln_stats(p, x_ap, nch, W, st, cols, deps):
    tacc = None
    for c in range(nch):
        p.add("act", "activation", st["sq"][:], x_ap[:, c, :], AF.Copy,
              accum_out=st["s1"][:, cols][:, c:c + 1], deps=deps)
        tacc = p.add("act", "activation", st["sq"][:], x_ap[:, c, :], AF.Square,
                     accum_out=st["s2"][:, cols][:, c:c + 1])
    s1, s2, m, v, r, nm, tmp = (st["s1"][:, cols], st["s2"][:, cols],
                                st["m"][:, cols], st["v"][:, cols],
                                st["r"][:, cols], st["nm"][:, cols],
                                st["tmp"][:, cols])
    p.add("dve", "tensor_scalar", m, s1, 1.0 / W, None, ALU.mult, deps=[tacc])
    p.add("dve", "tensor_scalar", v, s2, 1.0 / W, None, ALU.mult)
    p.add("dve", "tensor_tensor", tmp, m, m, ALU.mult)
    p.add("dve", "tensor_sub", v, v, tmp)
    p.add("dve", "tensor_scalar", v, v, LN_EPS, None, ALU.add)
    p.add("dve", "reciprocal", r, v)
    for _ in range(4):
        p.add("dve", "tensor_tensor", tmp, r, r, ALU.mult)
        p.add("dve", "tensor_tensor", tmp, tmp, v, ALU.mult)
        p.add("dve", "tensor_scalar", tmp, tmp, -0.5, 1.5, ALU.mult, ALU.add)
        p.add("dve", "tensor_tensor", r, r, tmp, ALU.mult)
    p.add("dve", "tensor_tensor", nm, m, r, ALU.mult)
    return p.add("dve", "tensor_scalar", nm, nm, -1.0, None, ALU.mult)


def make_qstate(ctx, nc, sbfn, prefix, QW, nsets):
    NB = QW // 16
    sets = []
    for r in range(nsets):
        sets.append({
            "am": sbfn(ctx, f"{prefix}am{r}", [128, NB], F32),
            "scale": sbfn(ctx, f"{prefix}sc{r}", [128, NB], F32),
            "recip": sbfn(ctx, f"{prefix}rc{r}", [128, NB], F32),
            "sclb": sbfn(ctx, f"{prefix}sb{r}", [128, NB], BF16),
            "y": sbfn(ctx, f"{prefix}y{r}", [128, QW], F32),
            "w1": sbfn(ctx, f"{prefix}w1{r}", [128, QW], F32),
            "A": sbfn(ctx, f"{prefix}A{r}", [128, QW], BF16),
            "B": sbfn(ctx, f"{prefix}B{r}", [128, QW], BF16),
            "sg": sbfn(ctx, f"{prefix}sg{r}", [128, QW], BF16),
        })
    return {"rot": 0, "sets": sets, "war": [dict() for _ in range(nsets)]}


def emit_quant(p, src_ap, out_ap, F, q, deps, out_war=None):
    r = q["rot"] % len(q["sets"])
    q["rot"] += 1
    b = q["sets"][r]
    war = q["war"][r]
    nb = F // 16
    src3 = src_ap.rearrange("p (nb s) -> p nb s", s=16)
    tred = p.add("dve", "tensor_reduce", b["am"][:, 0:nb], src3, AX.X, ALU.max,
                 apply_absolute_value=True, deps=deps + [war.get("am", [])])
    tsc = p.add("dve", "tensor_scalar", b["scale"][:, 0:nb], b["am"][:, 0:nb],
                1.0 / 6.0, 1e-12, ALU.mult, ALU.max, deps=[tred])
    tsb = p.add("dve", "tensor_copy", b["sclb"][:, 0:nb], b["scale"][:, 0:nb],
                deps=[war.get("sclb", [])])
    trc = p.add("dve", "reciprocal", b["recip"][:, 0:nb], b["scale"][:, 0:nb],
                deps=[war.get("recip", [])])
    tabs = p.add("act", "activation", b["y"][:, 0:F], src_ap, AF.Abs,
                 deps=deps + [war.get("y", [])])
    tsg = p.add("act", "activation", b["sg"][:, 0:F], src_ap, AF.Sign,
                deps=deps + [war.get("sg", [])])
    y3 = b["y"][:, 0:F].rearrange("p (nb s) -> p nb s", s=16)
    rb = b["recip"][:, 0:nb].rearrange("p nb -> p nb ()").broadcast_to([128, nb, 16])
    ty = p.add("dve", "tensor_tensor", y3, y3, rb, ALU.mult, deps=[tabs, trc])
    yF = b["y"][:, 0:F]
    wF = b["w1"][:, 0:F]
    aF = b["A"][:, 0:F]
    bF = b["B"][:, 0:F]
    p.add("dve", "tensor_scalar", wF, yF, 2.0, C_05, ALU.min, ALU.add,
          deps=[war.get("w1", [])])
    p.add("dve", "tensor_scalar", aF, wF, C_05, None, ALU.subtract,
          deps=[war.get("A", [])])
    p.add("dve", "tensor_scalar", wF, yF, 2.0, 4.0, ALU.max, ALU.min)
    tB = p.add("dve", "tensor_scalar", bF, wF, C_INT, C_INT + 2.0, ALU.add,
               ALU.subtract, deps=[war.get("B", [])])
    p.add("dve", "tensor_tensor", aF, aF, bF, ALU.add)
    tG = p.add("dve", "tensor_scalar", bF, yF, 5.0, 2.0, ALU.is_gt, ALU.mult)
    tlast = p.add("dve", "tensor_tensor", aF, aF, bF, ALU.add)
    tsgm = p.add("dve", "tensor_tensor", aF, aF, b["sg"][:, 0:F], ALU.mult,
                 deps=[tsg])
    a3 = b["A"][:, 0:F].rearrange("p (nb s) -> p nb s", s=16)
    o3 = out_ap.rearrange("p (nb s) -> p nb s", s=16)
    scb3 = b["sclb"][:, 0:nb].rearrange("p nb -> p nb ()").broadcast_to([128, nb, 16])
    tout = p.add("dve", "tensor_tensor", o3, a3, scb3, ALU.mult,
                 deps=[tsb, out_war])
    war["am"] = [tsc]
    war["sclb"] = [tout]
    war["recip"] = [ty]
    war["y"] = [tG]
    war["w1"] = [tB]
    war["sg"] = [tsgm]
    war["A"] = [tout]
    war["B"] = [tlast]
    return tout, [tred, tabs, tsg]



def build_program(S=8192, E=1024, I=2816):
    EC = E // 128
    NCH = S // 128
    NSLOT = NCH // 2
    NT = S // 256
    G = min(8, NSLOT)
    IC = I // 128
    IH = I // 2
    assert NSLOT % G == 0 and E % 128 == 0 and I % 256 == 0 and S % 1024 == 0
    NG = NSLOT // G
    HALFD = IC // 2
    GUH = (IC + 1) // 2
    iblocks = []
    off = 0
    while off < I:
        w = min(512, I - off)
        iblocks.append((off, w))
        off += w
    NIB = len(iblocks)
    NBMAX = max(E, IH, 512) // 16
    QW = max(E, IH, 512)

    nc = bass.Bass("TRN2", target_bir_lowering=False)

    xp = nc.dram_tensor("xp", [S, E], F32, kind="ExternalInput").ap()
    wqtd = nc.dram_tensor("wqt", [E, E], F32, kind="ExternalInput").ap()
    wktd = nc.dram_tensor("wkt", [E, E], F32, kind="ExternalInput").ap()
    wvtd = nc.dram_tensor("wvt", [E, E], F32, kind="ExternalInput").ap()
    bqd = nc.dram_tensor("bq", [128, EC], F32, kind="ExternalInput").ap()
    bkd = nc.dram_tensor("bk", [128, EC], F32, kind="ExternalInput").ap()
    bvd = nc.dram_tensor("bv", [1, E], F32, kind="ExternalInput").ap()
    wgd = nc.dram_tensor("wg", [I, E], F32, kind="ExternalInput").ap()
    wud = nc.dram_tensor("wu", [I, E], F32, kind="ExternalInput").ap()
    wdd = nc.dram_tensor("wd", [E, I], F32, kind="ExternalInput").ap()
    mskd = nc.dram_tensor("msk", [2, 128, 128], F32, kind="ExternalInput").ap()
    yd = nc.dram_tensor("y", [NSLOT, 128, E], F32, kind="ExternalOutput").ap()
    kfm = nc.dram_tensor("kfm", [EC, 128, S], BF16, kind="Internal").ap()
    vtm = nc.dram_tensor("vtm", [NCH, 128, E], BF16, kind="Internal").ap()
    qfm = nc.dram_tensor("qfm", [EC, 128, 128 * NSLOT], BF16, kind="Internal").ap()
    x1d = nc.dram_tensor("x1d", [NSLOT, 128, E], F32, kind="Internal").ap()
    wgtd = nc.dram_tensor("wgtd", [EC, 128, I], BF16, kind="Internal").ap()
    wutd = nc.dram_tensor("wutd", [EC, 128, I], BF16, kind="Internal").ap()
    wdtd = nc.dram_tensor("wdtd", [IC, 128, E], BF16, kind="Internal").ap()

    outer = contextlib.ExitStack()
    with outer:
        sems = {}
        for name in ("pe", "dve", "act", "pool", "d_init",
                     "xa0", "xa1", "xa2", "wl0", "wl1", "kst0", "kst1", "kk2",
                     "vst0", "vst1", "qst", "wfl",
                     "kk0", "kk1", "vv0", "vv1", "qgl", "xrl", "xs0", "xs1",
                     "wcl", "xl0", "xl1", "xl2", "ys0", "ys1"):
            sems[name] = outer.enter_context(nc.semaphore(name))
        state = Prog.fresh_state(sems)

        def sb(ctx, name, shape, dt):
            return ctx.enter_context(nc.sbuf_tensor(name, shape, dt))

        def psr(ctx, name, shape, dt):
            return ctx.enter_context(nc.psum_tensor(name, shape, dt))

        iden = sb(outer, "iden", [128, 128], BF16)
        cneg = sb(outer, "cneg", [128, 1], F32)
        ones_l = sb(outer, "ones_l", [128, 1], BF16)
        msk = sb(outer, "msk_sb", [128, 2, 128], F32)
        bq128 = sb(outer, "bq128", [128, EC], F32)
        bk128 = sb(outer, "bk128", [128, EC], F32)
        bq32 = sb(outer, "bq32", [128, EC], F32)

        with contextlib.ExitStack() as actx:
            p = Prog(nc, sems, state)
            xt = [sb(actx, f"xt{i}", [128, 2, E], F32) for i in range(3)]
            xln = [sb(actx, f"xln{i}", [128, 2, E], BF16) for i in range(2)]
            xfm = [sb(actx, f"xfm{i}", [128, EC, 256], BF16) for i in range(2)]
            ktb = [sb(actx, f"ktb{i}", [128, EC, 256], BF16) for i in range(2)]
            vtb = [sb(actx, f"vtb{i}", [128, 2, E], BF16) for i in range(2)]
            qtb = sb(actx, "qtb", [128, EC, 512], BF16)
            wqkv = {nm: sb(actx, nm + "t_sb", [128, EC, E], BF16)
                    for nm in ("wq", "wk", "wv")}
            wld = [sb(actx, f"wld{i}", [128, E], F32) for i in range(2)]
            st = {k: sb(actx, "ln_" + k, [128, 6], F32)
                  for k in ("s1", "s2", "m", "v", "r", "nm", "tmp")}
            st["sq"] = sb(actx, "ln_sq", [128, E], BF16)
            bvf = sb(actx, "bvf", [128, E], F32)
            sbv = sb(actx, "sbv", [1, E], F32)
            sbvb = sb(actx, "sbvb", [1, E], BF16)
            ones1 = sb(actx, "ones1", [1, 128], BF16)

            ptr = [psr(actx, f"ptr{i}", [128, 1024], BF16) for i in range(2)]
            kps = [psr(actx, f"kps{i}", [128, 512], F32) for i in range(2)]
            qps = [psr(actx, f"qps{i}", [128, 512], F32) for i in range(2)]
            vps = [psr(actx, f"vps{i}", [128, 512], F32) for i in range(2)]

            warb = {"ptr": {0: None, 1: None}, "kps": {0: None, 1: None},
                    "qps": {0: None, 1: None}, "vps": {0: None, 1: None}}
            p.add("pool", "memset", iden[:], 0.0)
            tid = p.add("pool", "affine_select", out=iden[:], in_=iden[:],
                        compare_op=ALU.not_equal, fill=1.0, base=0,
                        pattern=[[-1, 128]], channel_multiplier=1)
            p.add("pool", "memset", cneg[:], EXP_SHIFT)
            p.add("pool", "memset", ones_l[:], 1.0)
            t_ones = p.add("pool", "memset", ones1[:], 1.0)
            p.add("sp", "dma_start", msk[:], mskd.rearrange("i k q -> k i q"),
                  dsem="d_init")
            p.add("sp", "dma_start", bq128[:], bqd, dsem="d_init")
            p.add("sp", "dma_start", bk128[:], bkd, dsem="d_init")
            t_init = p.add("sp", "dma_start", sbv[:], bvd, dsem="d_init")
            t_msk = t_bq = t_bk = t_bv = t_init
            p.add("dve", "tensor_scalar", bq32[:], bq128[:], 1.0 / 32.0, None,
                  ALU.mult, deps=[t_bq])
            tbvb = p.add("dve", "tensor_copy", sbvb[:], sbv[:], deps=[t_bv])
            tbvf = None
            for h in range(0, E, 512):
                w = min(512, E - h)
                pi = (h // 512) % 2
                tmm = p.add("pe", "matmul", vps[pi][:, 0:w], ones1[:],
                            sbvb[:, h:h + w], start=True, stop=True,
                            deps=[tbvb, t_ones, warb["vps"][pi]])
                tbvf = p.add("dve", "tensor_copy", bvf[:, h:h + w],
                             vps[pi][:, 0:w], deps=[tmm])
                warb["vps"][pi] = tbvf

            wcast = {}
            widx = 0
            for nm, dsrc in (("wq", wqtd), ("wk", wktd), ("wv", wvtd)):
                for ec in range(EC):
                    buf = wld[widx % 2]
                    war = wcast.get(widx - 2)
                    tl = p.add("sp", "dma_start", buf[:, 0:E],
                               dsrc.rearrange("(ec pp) f -> ec pp f", pp=128)[ec],
                               deps=[war] if war else [], dsem=f"wl{widx % 2}")
                    wcast[widx] = p.add("dve", "tensor_copy", wqkv[nm][:, ec, :],
                                        buf[:, 0:E], deps=[tl])
                    widx += 1
            t_wq = wcast[EC - 1]
            t_wk = wcast[2 * EC - 1]
            t_wv = wcast[3 * EC - 1]

            xpt = xp.rearrange("(t c pp) e -> t pp c e", c=2, pp=128)
            tt = {}

            def a_load(t):
                if t >= NT:
                    return
                tt[("ld", t)] = p.add("sp", "dma_start", xt[t % 3][:], xpt[t],
                                      deps=[tt.get(("apply", t - 3))],
                                      dsem=f"xa{t % 3}")

            def a_stats(t):
                if t >= NT:
                    return
                cols = slice((t % 3) * 2, (t % 3) * 2 + 2)
                tt[("st", t)] = emit_ln_stats(p, xt[t % 3][:], 2, E, st, cols,
                                              [tt[("ld", t)],
                                               tt.get(("apply", t - 3))])

            def a_apply(t):
                if t >= NT:
                    return
                toks = []
                for c in range(2):
                    col = (t % 3) * 2 + c
                    dps = [tt[("st", t)]]
                    if c == 0:
                        dps.append(tt.get(("trdone", t - 2)))
                    toks.append(p.add("act", "activation", xln[t % 2][:, c, :],
                                      xt[t % 3][:, c, :], AF.Identity,
                                      bias=st["nm"][:, col:col + 1],
                                      scale=st["r"][:, col:col + 1], deps=dps))
                tt[("applies", t)] = toks
                tt[("apply", t)] = toks[-1]

            a_load(0); a_stats(0)
            a_load(1); a_stats(1)
            a_apply(0)

            for t in range(NT):
                bi = t % 2
                a_load(t + 2)
                apply_toks = tt[("applies", t)]
                dcp = None
                for c in range(2):
                    tts = None
                    for ec in range(EC):
                        dps = ()
                        if ec == 0:
                            dps = [apply_toks[c], tid, warb["ptr"][c]]
                        tts = p.add("pe", "transpose",
                                    ptr[c][:, ec * 128:(ec + 1) * 128],
                                    xln[bi][:, c, ec * 128:(ec + 1) * 128],
                                    iden[:], deps=dps, inc=(ec == EC - 1))
                    dcp = p.add("dve", "tensor_copy",
                                xfm[bi][:, :, c * 128:(c + 1) * 128],
                                ptr[c][:, 0:EC * 128].rearrange(
                                    "p (ec qq) -> p ec qq", qq=128),
                                deps=[tts, tt.get(("projdone", t - 2))])
                    warb["ptr"][c] = dcp
                tt[("trdone", t)] = dcp
                txfm = dcp
                kev = None
                for fc in range(EC):
                    tmm = None
                    for ec in range(EC):
                        dps = ()
                        if ec == 0:
                            dps = [txfm, t_wk, warb["kps"][fc % 2]]
                        tmm = p.add("pe", "matmul", kps[fc % 2][:, 0:256],
                                    wqkv["wk"][:, ec, fc * 128:(fc + 1) * 128],
                                    xfm[bi][:, ec, :],
                                    start=(ec == 0), stop=(ec == EC - 1),
                                    deps=dps, inc=(ec == EC - 1))
                    kev = p.add("act", "activation", ktb[bi][:, fc, :],
                                kps[fc % 2][:, 0:256], AF.Identity,
                                bias=bk128[:, fc:fc + 1],
                                deps=[tmm, t_bk, tt.get(("kst", t - 2))])
                    warb["kps"][fc % 2] = kev
                tt[("kst", t)] = p.add(
                    "pool", "dma_start",
                    kfm.rearrange("ec pp k -> pp ec k")[:, :, 256 * t:256 * (t + 1)],
                    ktb[bi][:], deps=[kev], dsem=f"kst{bi}")
                qev = None
                for fc in range(EC):
                    tmm = None
                    for ec in range(EC):
                        dps = ()
                        if ec == 0:
                            dps = [txfm, t_wq, warb["qps"][fc % 2]]
                        tmm = p.add("pe", "matmul", qps[fc % 2][:, 0:128],
                                    wqkv["wq"][:, ec, fc * 128:(fc + 1) * 128],
                                    xfm[bi][:, ec, 0:128],
                                    start=(ec == 0), stop=(ec == EC - 1),
                                    deps=dps, inc=(ec == EC - 1))
                    qev = p.add("act", "activation",
                                qtb[:, fc, (t % 4) * 128:(t % 4 + 1) * 128],
                                qps[fc % 2][:, 0:128], AF.Identity,
                                bias=bq32[:, fc:fc + 1], scale=1.0 / 32.0,
                                deps=[tmm, tt.get(("qst", 4 * (t // 4) - 1))])
                    warb["qps"][fc % 2] = qev
                if t % 4 == 3 or t == NT - 1:
                    u = t // 4
                    ncols = (t % 4 + 1) * 128
                    tt[("qst", t)] = p.add(
                        "pool", "dma_start",
                        qfm.rearrange("ec pp k -> pp ec k")[:, :, 512 * u:512 * u + ncols],
                        qtb[:, :, 0:ncols], deps=[qev], dsem="qst")
                vev = None
                vpi = 0
                for c in range(2):
                    for h in range(0, E, 512):
                        w = min(512, E - h)
                        pi = vpi % 2
                        vpi += 1
                        tmm = None
                        for ec in range(EC):
                            dps = ()
                            if ec == 0:
                                dps = [txfm, t_wv, tbvf, warb["vps"][pi]]
                            tmm = p.add("pe", "matmul", vps[pi][:, 0:w],
                                        xfm[bi][:, ec, c * 128:(c + 1) * 128],
                                        wqkv["wv"][:, ec, h:h + w],
                                        start=(ec == 0), stop=(ec == EC - 1),
                                        deps=dps, inc=(ec == EC - 1))
                        vev = p.add("dve", "tensor_tensor", vtb[bi][:, c, h:h + w],
                                    vps[pi][:, 0:w], bvf[:, h:h + w], ALU.add,
                                    deps=[tmm, tbvf, tt.get(("vst", t - 2))])
                        warb["vps"][pi] = vev
                tt[("vst", t)] = p.add(
                    "pool", "dma_start",
                    vtm.rearrange("kc pp e -> pp kc e")[:, 2 * t:2 * t + 2, :],
                    vtb[bi][:], deps=[vev], dsem=f"vst{bi}")
                tt[("projdone", t)] = vev
                tt[("lastuse", t)] = vev
                a_stats(t + 2)
                a_apply(t + 1)

            blk = actx.enter_context(nc.Block())
            p.emit(blk)

        with contextlib.ExitStack() as bctx:
            p = Prog(nc, sems, state)
            p.barrier()
            qg = sb(bctx, "qg", [128, EC, 128 * G], BF16)
            kk = [sb(bctx, f"kk{i}", [128, EC, 512], BF16) for i in range(3)]
            vv = [sb(bctx, f"vv{i}", [128, 4, E], BF16) for i in range(2)]
            og = sb(bctx, "og", [128, G, E + 1], F32)
            pt_all = [sb(bctx, f"pta{i}", [128, 4, 512], BF16) for i in range(2)]
            xr = sb(bctx, "xr", [128, E], F32)
            x1t = [sb(bctx, f"x1t{i}", [128, E], F32) for i in range(2)]
            lsb = sb(bctx, "lsb", [128, G], F32)
            linv = sb(bctx, "linv", [128, G], F32)
            wldb = [sb(bctx, f"wldb{i}", [128, max(E, IH)], F32)
                    for i in range(2)]
            wqbb = [sb(bctx, f"wqbb{i}", [128, max(E, IH)], BF16)
                    for i in range(2)]
            wstage = sb(bctx, "wstage", [128, E * IC // 2], BF16)
            qa = make_qstate(bctx, nc, sb, "qa_", QW, 2)
            stq = [psr(bctx, f"stq{i}", [128, 512], F32) for i in range(2)]
            avp = [psr(bctx, f"avp{i}", [128, E], F32) for i in range(2)]
            lps = psr(bctx, "lps", [128, 512], F32)
            wptr = psr(bctx, "wptr", [128, 1024], BF16)

            bt = {}
            sgi = 0
            bct = 0
            sit = 0
            itlast = {}
            pend_prev = []

            def emit_tail(tc):
                s_it, ar, kbi, nkc, kt, jj, g_, pb, boff, texp, tvl = (
                    tc["sit"], tc["ar"], tc["kbi"], tc["nkc"], tc["kt"],
                    tc["jj"], tc["g"], tc["pb"], tc["boff"], tc["texp"], tc["tvl"])
                tav = None
                for h in range(0, E, 512):
                    w = min(512, E - h)
                    for kc in range(nkc):
                        dps = ()
                        if kc == 0 and h == 0:
                            dps = [texp, tvl, bt.get(("og", s_it - 2))]
                        tav = p.add("pe", "matmul", avp[ar][:, h:h + w],
                                    pt_all[pb][:, kc, boff:boff + 128],
                                    vv[kbi][:, kc, h:h + w],
                                    start=(kc == 0), stop=(kc == nkc - 1),
                                    skip_group_check=True,
                                    deps=dps, inc=False)
                lcol = s_it
                assert lcol < 512
                tl_ = None
                for kc in range(nkc):
                    tl_ = p.add("pe", "matmul", lps[:, lcol:lcol + 1],
                                pt_all[pb][:, kc, boff:boff + 128],
                                ones_l[:],
                                start=(kc == 0), stop=(kc == nkc - 1),
                                skip_group_check=True,
                                inc=(kc == nkc - 1))
                bt[("l", s_it)] = tl_
                ogkey = ("ogslot", g_, jj)
                if kt == 0:
                    tog = p.add("dve", "tensor_copy", og[:, jj, 0:E],
                                avp[ar][:, 0:E],
                                deps=[tl_, bt.get(("fin", g_ - 1, jj))])
                    tog = p.add("dve", "tensor_copy", og[:, jj, E:E + 1],
                                lps[:, lcol:lcol + 1])
                else:
                    tog = p.add("dve", "tensor_add", og[:, jj, 0:E],
                                og[:, jj, 0:E], avp[ar][:, 0:E],
                                deps=[tl_, bt.get(ogkey)])
                    tog = p.add("dve", "tensor_add", og[:, jj, E:E + 1],
                                og[:, jj, E:E + 1], lps[:, lcol:lcol + 1])
                bt[ogkey] = tog
                bt[("og", s_it)] = tog

            wjobs = ([("wg", None, ti) for ti in range(IC)] +
                     [("wu", None, ti) for ti in range(IC)] +
                     [("wd", h, ti) for h in range(2) for ti in range(EC)])
            nwj = len(wjobs)
            wj = {"qtok": {}, "srel": {}, "wqb_war": {}, "stage_war": None,
                  "ptr_war": None}

            def wjob_head(idx):
                nm, h, ti = wjobs[idx]
                bi = idx % 2
                if nm == "wd":
                    F_ = IH
                    src_ = wdd.rearrange("(t pp) i -> t pp i", pp=128)[ti][:, IH * h:IH * (h + 1)]
                else:
                    F_ = E
                    dsrc = wgd if nm == "wg" else wud
                    src_ = dsrc.rearrange("(t pp) f -> t pp f", pp=128)[ti]
                tl = p.add("act", "dma_start", wldb[bi][:, 0:F_], src_,
                           deps=[wj["srel"].get(idx - 2, [])], dsem=f"wl{bi}")
                tq, srel = emit_quant(p, wldb[bi][:, 0:F_], wqbb[bi][:, 0:F_],
                                      F_, qa, [tl],
                                      out_war=wj["wqb_war"].get(bi))
                wj["qtok"][idx] = tq
                wj["srel"][idx] = srel

            def wjob_tail(idx):
                nm, h, ti = wjobs[idx]
                bi = idx % 2
                F_ = IH if nm == "wd" else E
                tq = wj["qtok"][idx]
                nblk = F_ // 128
                base = 0
                last_cp = None
                while base < nblk:
                    cnt = min(8, nblk - base)
                    tts = None
                    for b_ in range(cnt):
                        tts = p.add("pe", "transpose",
                                    wptr[:, b_ * 128:(b_ + 1) * 128],
                                    wqbb[bi][:, (base + b_) * 128:(base + b_ + 1) * 128],
                                    iden[:],
                                    deps=([tq, wj["ptr_war"]]
                                          if b_ == 0 else ()),
                                    inc=(b_ == cnt - 1))
                    srcv = wptr[:, 0:cnt * 128].rearrange(
                        "p (b_ qq) -> p b_ qq", qq=128)
                    if nm == "wd":
                        dst = wstage[:].rearrange("p (ic e) -> p ic e", e=E)[
                            :, base:base + cnt, ti * 128:(ti + 1) * 128]
                    else:
                        loc = ti % GUH
                        dst = wstage[:].rearrange("p (ec i) -> p ec i", ec=EC)[
                            :, :, loc * 128:(loc + 1) * 128]
                    stage_deps = [tts]
                    if wj["stage_war"] is not None:
                        stage_deps.append(wj["stage_war"])
                    last_cp = p.add("dve", "tensor_copy", dst, srcv,
                                    deps=stage_deps)
                    wj["ptr_war"] = last_cp
                    base += cnt
                wj["wqb_war"][bi] = last_cp
                if nm == "wd":
                    if ti == EC - 1:
                        dst = wdtd.rearrange("ic pp e -> pp ic e")[
                            :, h * HALFD:(h + 1) * HALFD, :]
                        srcd = wstage[:].rearrange("p (ic e) -> p ic e", e=E)[
                            :, 0:HALFD, :]
                        wj["stage_war"] = p.add("pool", "dma_start", dst, srcd,
                                                deps=[last_cp], dsem="wfl")
                else:
                    loc = ti % GUH
                    if loc == GUH - 1 or ti == IC - 1:
                        hidx = ti // GUH
                        base_i = hidx * GUH * 128
                        ncols = (loc + 1) * 128
                        ddst = wgtd if nm == "wg" else wutd
                        dst = ddst.rearrange("ec pp i -> pp ec i")[
                            :, :, base_i:base_i + ncols]
                        srcd = wstage[:].rearrange("p (ec i) -> p ec i", ec=EC)[
                            :, :, 0:ncols]
                        wj["stage_war"] = p.add("pool", "dma_start", dst, srcd,
                                                deps=[last_cp], dsem="wfl")

            NKT_TOT = sum((256 * (gg * G + G) + 511) // 512 for gg in range(NG))
            wji = 0
            gkt = 0

            for g in range(NG):
                s0 = g * G
                tqg = p.add("sp", "dma_start", qg[:],
                            qfm.rearrange("ec pp k -> pp ec k")[:, :, 128 * s0:128 * (s0 + G)],
                            deps=[bt.get(("qcons", g - 1))], dsem="qgl")
                nkt = (256 * (s0 + G) + 511) // 512
                for kt in range(nkt):
                    kbi = kt % 2
                    kki = kt % 3
                    prev2 = itlast.get((g, kt - 2), itlast.get((g - 1, "last")))
                    kwar = bt.get(("l", prev2)) if prev2 is not None else None
                    prev3 = itlast.get((g, kt - 3), itlast.get((g - 1, "last")))
                    kwar3 = bt.get(("l", prev3)) if prev3 is not None else None
                    tkl = p.add("sp", "dma_start", kk[kki][:],
                                kfm.rearrange("ec pp k -> pp ec k")[:, :, 512 * kt:512 * (kt + 1)],
                                deps=[kwar3], dsem=f"kk{kki}")
                    tvl = p.add("sp", "dma_start", vv[kbi][:],
                                vtm.rearrange("kc pp e -> pp kc e")[:, 4 * kt:4 * kt + 4, :],
                                deps=[kwar], dsem=f"vv{kbi}")
                    full, single = [], []
                    for jj in range(G):
                        keys = 256 * (s0 + jj + 1) - 512 * kt
                        if keys <= 0:
                            continue
                        (full if keys > 512 else single).append(jj)
                    batches = [full[i:i + 4] for i in range(0, len(full), 4)]
                    batches += [[jj] for jj in single]
                    for bslots in batches:
                        jj0, w = bslots[0], len(bslots)
                        assert bslots == list(range(jj0, jj0 + w))
                        keys0 = 256 * (s0 + bslots[-1] + 1) - 512 * kt
                        nkc = min(4, max(k_ // 128 for k_ in
                                         [256 * (s0 + j_ + 1) - 512 * kt
                                          for j_ in bslots]))
                        is_last = keys0 <= 512
                        pb = bct % 2
                        bct += 1
                        texp = None
                        for kc in range(nkc):
                            srot = sgi % 2
                            sgi += 1
                            tmm = None
                            for ec in range(EC):
                                dps = ()
                                if ec == 0:
                                    dps = [tkl, tqg, bt.get(("sexp", sgi - 1 - 2))]
                                tmm = p.add("pe", "matmul",
                                            stq[srot][:, 0:w * 128],
                                            kk[kki][:, ec, kc * 128:(kc + 1) * 128],
                                            qg[:, ec, (jj0) * 128:(jj0 + w) * 128],
                                            start=(ec == 0), stop=(ec == EC - 1),
                                            deps=dps,
                                            inc=(ec == EC - 1))
                            bt[("sc", sgi - 1)] = tmm
                            tsc = tmm
                            if is_last and w == 1 and kc >= nkc - 2:
                                mi = kc - (nkc - 2)
                                tsc = p.add("dve", "tensor_tensor",
                                            stq[srot][:, 0:128],
                                            stq[srot][:, 0:128],
                                            msk[:, mi, :], ALU.add,
                                            deps=[tsc, t_msk])
                            texp = p.add("act", "activation",
                                         pt_all[pb][:, kc, 0:w * 128],
                                         stq[srot][:, 0:w * 128], AF.Exp,
                                         bias=cneg[:],
                                         deps=[tsc, bt.get(("btail", bct - 1 - 2))])
                            bt[("sexp", sgi - 1)] = texp
                        newtails = []
                        for bi_, jj in enumerate(bslots):
                            keys = 256 * (s0 + jj + 1) - 512 * kt
                            newtails.append(dict(
                                sit=sit, ar=sit % 2, kbi=kbi,
                                nkc=min(4, keys // 128), kt=kt, jj=jj, g=g,
                                pb=pb, boff=bi_ * 128, texp=texp, tvl=tvl))
                            sit += 1
                        for tc in pend_prev:
                            emit_tail(tc)
                        if pend_prev:
                            bt[("btail", pend_prev[0]["bct"])] = bt[("l", pend_prev[-1]["sit"])]
                        for tc in newtails:
                            tc["bct"] = bct - 1
                        pend_prev = newtails
                        nb_ = len(batches)
                        bix_ = batches.index(bslots) + 1
                        tgt = (gkt * nb_ + bix_) * nwj // (NKT_TOT * nb_)
                        while wji < nwj and wji <= tgt - 1:
                            if wji >= 2:
                                wjob_tail(wji - 2)
                            wjob_head(wji)
                            wji += 1
                    itlast[(g, kt)] = sit - 1
                    gkt += 1
                itlast[(g, "last")] = sit - 1
                for tc in pend_prev:
                    emit_tail(tc)
                if pend_prev:
                    bt[("btail", pend_prev[0]["bct"])] = bt[("l", pend_prev[-1]["sit"])]
                pend_prev = []
                bt[("qcons", g)] = bt[("sc", sgi - 1)]
                ogdone = [bt.get(("ogslot", g, jj)) for jj in range(G)]
                tlev = p.add("dve", "tensor_copy",
                             lsb[:].rearrange("p g -> p g ()"),
                             og[:, :, E:E + 1],
                             deps=[bt[("og", sit - 1)]] + ogdone)
                p.add("dve", "reciprocal", linv[:], lsb[:])
                for jj in range(G):
                    j = s0 + jj
                    txr = p.add("sp", "dma_start", xr[:],
                                xp.rearrange("(n pp) e -> n pp e", pp=128)[2 * j],
                                deps=[bt.get("lastfin")], dsem="xrl")
                    p.add("dve", "tensor_scalar", x1t[jj % 2][:],
                          og[:, jj, 0:E], linv[:, jj:jj + 1], None, ALU.mult,
                          deps=[bt.get(("x1war", jj % 2)),
                                bt.get(("ogslot", g, jj))])
                    tfin = p.add("dve", "tensor_add", x1t[jj % 2][:],
                                 x1t[jj % 2][:], xr[:], deps=[txr])
                    bt[("fin", g, jj)] = tfin
                    bt["lastfin"] = tfin
                    bt[("x1war", jj % 2)] = p.add(
                        "pool", "dma_start", x1d[j], x1t[jj % 2][:],
                        deps=[tfin], dsem=f"xs{jj % 2}")

            while wji < nwj:
                if wji >= 2:
                    wjob_tail(wji - 2)
                wjob_head(wji)
                wji += 1
            for j_ in range(max(0, nwj - 2), nwj):
                wjob_tail(j_)

            blk = bctx.enter_context(nc.Block())
            p.emit(blk)

        with contextlib.ExitStack() as cctx:
            p = Prog(nc, sems, state)
            p.barrier()
            wgt = sb(cctx, "wgt_sb", [128, EC, I], BF16)
            wut = sb(cctx, "wut_sb", [128, EC, I], BF16)
            wdt = sb(cctx, "wdt_sb", [128, IC, E], BF16)
            x1b = [sb(cctx, f"x1b{i}", [128, E], F32) for i in range(2)]
            xlnc = sb(cctx, "xlnc", [128, E], F32)
            xqb = sb(cctx, "xqb", [128, E], BF16)
            xqf = [sb(cctx, f"xqf{i}", [128, EC, 128], BF16) for i in range(2)]
            sil = [sb(cctx, f"sil{i}", [128, 512], F32) for i in range(2)]
            hbuf = [sb(cctx, f"hbuf{i}", [128, 1024], F32) for i in range(2)]
            hqb = sb(cctx, "hqb", [128, 1024], BF16)
            hqf = [sb(cctx, f"hqf{i}", [128, 8, 128], BF16) for i in range(2)]
            ytb = sb(cctx, "ytb", [128, E], F32)
            stc = {k: sb(cctx, "lnc_" + k, [128, 3], F32)
                   for k in ("s1", "s2", "m", "v", "r", "nm", "tmp")}
            stc["sq"] = sb(cctx, "lnc_sq", [128, E], BF16)
            qc = make_qstate(cctx, nc, sb, "qc_", max(E, 512), 2)
            ptc = [psr(cctx, "ptc0", [128, 1024], BF16)]
            gps = [psr(cctx, f"gps{i}", [128, 512], F32) for i in range(3)]
            ups = [psr(cctx, f"ups{i}", [128, 512], F32) for i in range(2)]
            dps_ = psr(cctx, "dps", [128, max(E, 512)], F32)

            p.add("sp", "dma_start", wgt[:],
                  wgtd.rearrange("ec pp i -> pp ec i"), dsem="wcl")
            p.add("act", "dma_start", wut[:],
                  wutd.rearrange("ec pp i -> pp ec i"), dsem="wcl")
            twd = p.add("pool", "dma_start", wdt[:],
                        wdtd.rearrange("ic pp e -> pp ic e"), dsem="wcl")
            twg = twu = twd

            ct = {}
            trc = {"n": 0}

            def c_load(s):
                if s >= NSLOT:
                    return
                ct[("ld", s)] = p.add("sp", "dma_start", x1b[s % 2][:], x1d[s],
                                      deps=[ct.get(("res", s - 2))],
                                      dsem=f"xl{s % 3}")

            def c_stats(s):
                if s >= NSLOT:
                    return
                x1v = x1b[s % 2][:].rearrange("p (c e) -> p c e", c=1)
                ct[("st", s)] = emit_ln_stats(p, x1v, 1, E, stc,
                                              slice(s % 3, s % 3 + 1),
                                              [ct[("ld", s)],
                                               ct.get(("ap", s - 3))])

            def c_stage1(s):
                if s >= NSLOT:
                    return
                bi = s % 2
                tap = p.add("act", "activation", xlnc[:], x1b[s % 2][:],
                            AF.Identity, bias=stc["nm"][:, s % 3:s % 3 + 1],
                            scale=stc["r"][:, s % 3:s % 3 + 1],
                            deps=[ct[("st", s)], ct.get(("xqdone", s - 1))])
                ct[("ap", s)] = tap
                tq, srel = emit_quant(p, xlnc[:], xqb[:], E, qc,
                                      [tap], out_war=ct.get(("xqtr", s - 1)))
                ct[("xq", s)] = tq
                ct[("xqdone", s)] = srel

            def c_stage1_tr(s):
                if s >= NSLOT:
                    return
                bi = s % 2
                ri = 0
                tts = None
                for ec in range(EC):
                    dps = ()
                    if ec == 0:
                        dps = [ct[("xq", s)], tid, ct.get(("ptc", ri))]
                    tts = p.add("pe", "transpose",
                                ptc[ri][:, ec * 128:(ec + 1) * 128],
                                xqb[:, ec * 128:(ec + 1) * 128], iden[:],
                                deps=dps, inc=(ec == EC - 1))
                ct[("xqtr", s)] = tts
                txqf = p.add("act", "activation", xqf[bi][:].rearrange(
                                 "p ec qq -> p (ec qq)"),
                             ptc[ri][:, 0:EC * 128], AF.Copy,
                             deps=[tts, ct.get(("xqfcons", s - 2))])
                ct[("ptc", ri)] = txqf
                ct[("xqf", s)] = txqf

            c_load(0); c_stats(0)
            c_load(1); c_stats(1)
            c_stage1(0); c_stage1_tr(0)

            hgroups = []
            off = 0
            while off < I:
                w = min(1024, I - off)
                hgroups.append((off, w))
                off += w
            NHG = len(hgroups)

            def chunk_head(s, gib, hb, loc, ioff, cw):
                bi = s % 2
                bb = gib % 2
                hbr = hb % 2
                gp, up_ = gps[gib % 3], ups[bb]
                tg = None
                for ec in range(EC):
                    dps = ()
                    if ec == 0:
                        dps = [ct[("xqf", s)], twg, ct.get(("sil", gib - 3))]
                    tg = p.add("pe", "matmul", gp[:, 0:cw],
                               xqf[bi][:, ec, :],
                               wgt[:, ec, ioff:ioff + cw],
                               start=(ec == 0), stop=(ec == EC - 1),
                               deps=dps, inc=(ec == EC - 1))
                ct[("gate_last", s)] = tg
                tu = None
                for ec in range(EC):
                    dps = ()
                    if ec == 0:
                        dps = [twu, ct.get(("h", gib - 2))]
                    tu = p.add("pe", "matmul", up_[:, 0:cw],
                               xqf[bi][:, ec, :],
                               wut[:, ec, ioff:ioff + cw],
                               start=(ec == 0), stop=(ec == EC - 1),
                               deps=dps, inc=(ec == EC - 1))
                ct[("up_last", s)] = tu
                tsil = p.add("act", "activation", sil[bb][:, 0:cw],
                             gp[:, 0:cw], AF.Silu,
                             deps=[tg, ct.get(("h", gib - 2))])
                ct[("sil", gib)] = tsil
                th = p.add("dve", "tensor_tensor", hbuf[hbr][:, loc:loc + cw],
                           sil[bb][:, 0:cw], up_[:, 0:cw], ALU.mult,
                           deps=[tsil, tu, ct.get(("hrel", hb - 2))])
                ct[("h", gib)] = th
                return th

            def group_tail(hb):
                qoff, qw = hgroups[hb % NHG]
                hbr = hb % 2
                nkc = qw // 128
                ri = 0
                tts2 = None
                for kc in range(nkc):
                    dps = ()
                    if kc == 0:
                        dps = [ct[("hq", hb)], ct.get(("ptc", ri))]
                    tts2 = p.add("pe", "transpose",
                                 ptc[ri][:, kc * 128:(kc + 1) * 128],
                                 hqb[:, kc * 128:(kc + 1) * 128],
                                 iden[:], deps=dps, inc=(kc == nkc - 1))
                ct[("hqtr", hb)] = tts2
                thqf = p.add("act", "activation",
                             hqf[hbr][:, 0:nkc, :].rearrange(
                                 "p kc qq -> p (kc qq)"),
                             ptc[ri][:, 0:nkc * 128], AF.Copy,
                             deps=[tts2, ct.get(("hqfcons", hb - 2))])
                ct[("ptc", ri)] = thqf
                s = hb // NHG
                qi = hb % NHG
                td = None
                for kc in range(nkc):
                    icg = qoff // 128 + kc
                    for h in range(0, E, 512):
                        w = min(512, E - h)
                        dps = ()
                        if kc == 0 and h == 0:
                            dps = [thqf, twd]
                            if qi == 0:
                                dps.append(ct.get(("res", s - 1)))
                        td = p.add("pe", "matmul", dps_[:, h:h + w],
                                   hqf[hbr][:, kc, :],
                                   wdt[:, icg, h:h + w],
                                   start=(qi == 0 and kc == 0),
                                   stop=(qi == NHG - 1 and kc == nkc - 1),
                                   skip_group_check=True,
                                   deps=dps,
                                   inc=(kc == nkc - 1 and h + w >= E))
                ct[("hqfcons", hb)] = td
                if qi == NHG - 1:
                    ct[("down", s)] = td

            def residual(s):
                bi = s % 2
                tres = p.add("dve", "tensor_add", ytb[:], x1b[s % 2][:],
                             dps_[:, 0:E],
                             deps=[ct[("down", s)], ct.get(("yst", s - 1))])
                ct[("res", s)] = tres
                ct[("yst", s)] = p.add("pool", "dma_start", yd[s], ytb[:],
                                       deps=[tres], dsem=f"ys{bi}")

            gib = 0
            for s in range(NSLOT):
                for qi, (qoff, qw) in enumerate(hgroups):
                    hb = s * NHG + qi
                    hbr = hb % 2
                    ths = []
                    loc = 0
                    while loc < qw:
                        cw = min(512, qw - loc)
                        ths.append(chunk_head(s, gib, hb, loc, qoff + loc, cw))
                        gib += 1
                        loc += cw
                    if hb >= 1:
                        group_tail(hb - 1)
                    if qi == 2:
                        c_stage1_tr(s + 1)
                    thq, hrel = emit_quant(p, hbuf[hbr][:, 0:qw],
                                           hqb[:, 0:qw], qw, qc, ths,
                                           out_war=ct.get(("hqtr", hb - 1)))
                    ct[("hrel", hb)] = hrel
                    ct[("hq", hb)] = thq
                    if hb >= 1 and (hb - 1) % NHG == NHG - 1:
                        residual((hb - 1) // NHG)
                        c_load((hb - 1) // NHG + 2)
                        c_stats((hb - 1) // NHG + 2)
                    if qi == 1:
                        c_stage1(s + 1)
                ct[("xqfcons", s)] = ct[("up_last", s)]
            group_tail(NSLOT * NHG - 1)
            residual(NSLOT - 1)

            p.wait_only("pool", [Tok("ys0", p.state["count"]["ys0"]),
                                 Tok("ys1", p.state["count"]["ys1"])])
            blk = cctx.enter_context(nc.Block())
            p.emit(blk)

    nc.finalize()
    return nc



def make_core_inputs(x, Wq, bq, Wk, bk, Wv, bv, wg, wu, wd, S, E, I, core):
    n, par = core // 2, core % 2
    xv = np.ascontiguousarray(x[n], dtype=np.float32)
    if par == 1:
        xv = np.ascontiguousarray(
            xv.reshape(S // 256, 2, 128, E)[:, ::-1].reshape(S, E))
    tri = np.where(np.arange(128)[None, :] >= np.arange(128)[:, None],
                   np.float32(0.0), np.float32(NEG_BIG))
    other = np.full((128, 128), (NEG_BIG if par == 0 else 0.0), np.float32)
    msk = np.stack([tri, other]).astype(np.float32)
    return {
        "xp": xv,
        "wqt": np.ascontiguousarray(Wq.T, dtype=np.float32),
        "wkt": np.ascontiguousarray(Wk.T, dtype=np.float32),
        "wvt": np.ascontiguousarray(Wv.T, dtype=np.float32),
        "bq": np.ascontiguousarray(bq.reshape(E // 128, 128).T, dtype=np.float32),
        "bk": np.ascontiguousarray(bk.reshape(E // 128, 128).T, dtype=np.float32),
        "bv": np.ascontiguousarray(bv.reshape(1, E), dtype=np.float32),
        "wg": np.ascontiguousarray(wg, dtype=np.float32),
        "wu": np.ascontiguousarray(wu, dtype=np.float32),
        "wd": np.ascontiguousarray(wd, dtype=np.float32),
        "msk": msk,
    }


def assemble_output(results, S, E, B=4):
    out = np.empty((B, S, E), np.float32)
    for core in range(2 * B):
        n, par = core // 2, core % 2
        yv = np.asarray(results[core]["y"])
        out[n].reshape(S // 256, 2, 128, E)[:, par] = yv
    return out


def kernel(L0_attn_x, L0_attn_Wq, L0_attn_bq, L0_attn_Wk, L0_attn_bk,
           L0_attn_Wv, L0_attn_bv, L0_ffn_w_gate, L0_ffn_w_up, L0_ffn_w_down):
    S, E, I = 8192, 1024, 2816
    x = np.asarray(L0_attn_x, dtype=np.float32)
    Wq, bq, Wk, bk, Wv, bv, wg, wu, wd = [
        np.asarray(a, dtype=np.float32) for a in
        (L0_attn_Wq, L0_attn_bq, L0_attn_Wk, L0_attn_bk, L0_attn_Wv,
         L0_attn_bv, L0_ffn_w_gate, L0_ffn_w_up, L0_ffn_w_down)]
    nc = build_program(S, E, I)
    in_maps = [make_core_inputs(x, Wq, bq, Wk, bk, Wv, bv, wg, wu, wd,
                                S, E, I, core) for core in range(8)]
    res = run_bass_kernel_spmd(nc, in_maps, core_ids=list(range(8)))
    return assemble_output(res.results, S, E)


# version 48 (final)
# speedup vs baseline: 1.1880x; 1.0073x over previous
import contextlib
import numpy as np
import concourse.bass as bass
import concourse.mybir as mybir
from concourse.bass_utils import run_bass_kernel_spmd

F32 = mybir.dt.float32
BF16 = mybir.dt.bfloat16
AF = mybir.ActivationFunctionType
ALU = mybir.AluOpType
AX = mybir.AxisListType

C_INT = 1.5 * 2**23
C_05 = 1.5 * 2**22
NEG_BIG = -1.0e9
EXP_SHIFT = -20.0
LN_EPS = 1e-5

SECTION = {"pe": "tensor", "dve": "vector", "act": "scalar", "pool": "gpsimd", "sp": "sync"}
CHAINED = ("dve", "act", "pool")


class Tok:
    __slots__ = ("sem", "val")
    def __init__(self, sem, val):
        self.sem = sem
        self.val = val


class Prog:

    def __init__(self, nc, sems, state):
        self.nc = nc
        self.sems = sems
        self.state = state
        self.ops = {e: [] for e in SECTION}

    @classmethod
    def fresh_state(cls, sems):
        return {
            "count": {name: 0 for name in sems},
            "hw": {e: {} for e in SECTION},
            "last": {e: None for e in SECTION},
            "barrier": {e: [] for e in SECTION},
        }

    def _tok(self, semname, amt):
        self.state["count"][semname] += amt
        return Tok(semname, self.state["count"][semname])

    def add(self, eng, method, *args, deps=(), inc=True, **kw):
        flat = []
        for d in deps:
            if d is None:
                continue
            if isinstance(d, list):
                flat.extend(x for x in d if x is not None)
            else:
                flat.append(d)
        deps = flat
        if self.state["barrier"][eng]:
            deps = self.state["barrier"][eng] + deps
            self.state["barrier"][eng] = []
        is_dma = method == "dma_start"
        dsem = kw.pop("dsem", None)
        if eng in CHAINED and not is_dma:
            if self.state["last"][eng] is not None:
                deps.append(self.state["last"][eng])
        if is_dma:
            assert dsem is not None, "dma_start needs dsem="
            tok = self._tok(dsem, 16)
            incspec = (dsem, 16)
        elif inc:
            tok = self._tok(eng, 1)
            incspec = (eng, 1)
        else:
            tok = None
            incspec = None
        self.ops[eng].append((method, args, kw, deps, incspec))
        if eng in CHAINED and not is_dma and tok is not None:
            self.state["last"][eng] = tok
        return tok

    def wait_only(self, eng, deps):
        self.ops[eng].append((None, (), {}, list(deps), None))

    def barrier(self):
        toks = [Tok(name, cnt) for name, cnt in self.state["count"].items() if cnt > 0]
        for e in SECTION:
            self.state["barrier"][e] = list(toks)

    def emit(self, blk):
        for ename, ops in self.ops.items():
            if not ops:
                continue
            def make(ops=ops, ename=ename):
                def body(engine):
                    hw = self.state["hw"][ename]
                    for (method, args, kw, deps, incspec) in ops:
                        for d in deps:
                            if hw.get(d.sem, -1) < d.val:
                                engine.wait_ge(self.sems[d.sem], d.val)
                                hw[d.sem] = d.val
                        if method is None:
                            continue
                        instr = getattr(engine, method)(*args, **kw)
                        if incspec is not None:
                            instr.then_inc(self.sems[incspec[0]], incspec[1])
                return body
            getattr(blk, SECTION[ename])(make())



def emit_ln(p, x_ap, out_ap, nch, W, st, deps, war_out=None):
    tacc = None
    for c in range(nch):
        p.add("act", "activation", st["sq"][:], x_ap[:, c, :], AF.Copy,
              accum_out=st["s1"][:, c:c + 1], deps=deps)
        tacc = p.add("act", "activation", st["sq"][:], x_ap[:, c, :], AF.Square,
                     accum_out=st["s2"][:, c:c + 1])
    s1, s2, m, v, r, nm, tmp = (st["s1"], st["s2"], st["m"], st["v"], st["r"],
                                st["nm"], st["tmp"])
    cc = slice(0, nch)
    p.add("dve", "tensor_scalar", m[:, cc], s1[:, cc], 1.0 / W, None, ALU.mult,
          deps=[tacc])
    p.add("dve", "tensor_scalar", v[:, cc], s2[:, cc], 1.0 / W, None, ALU.mult)
    p.add("dve", "tensor_tensor", tmp[:, cc], m[:, cc], m[:, cc], ALU.mult)
    p.add("dve", "tensor_sub", v[:, cc], v[:, cc], tmp[:, cc])
    p.add("dve", "tensor_scalar", v[:, cc], v[:, cc], LN_EPS, None, ALU.add)
    p.add("dve", "memset", r[:, cc], 1.0)
    for _ in range(5):
        p.add("dve", "tensor_tensor", tmp[:, cc], r[:, cc], r[:, cc], ALU.mult)
        p.add("dve", "tensor_tensor", tmp[:, cc], tmp[:, cc], v[:, cc], ALU.mult)
        p.add("dve", "tensor_scalar", tmp[:, cc], tmp[:, cc], -0.5, 1.5,
              ALU.mult, ALU.add)
        p.add("dve", "tensor_tensor", r[:, cc], r[:, cc], tmp[:, cc], ALU.mult)
    p.add("dve", "tensor_tensor", nm[:, cc], m[:, cc], r[:, cc], ALU.mult)
    trstd = p.add("dve", "tensor_scalar", nm[:, cc], nm[:, cc], -1.0, None, ALU.mult)
    toks = []
    for c in range(nch):
        dps = [trstd] + list(deps)
        if war_out is not None and c == 0:
            dps.append(war_out)
        toks.append(p.add("act", "activation", out_ap[:, c, :], x_ap[:, c, :],
                          AF.Identity, bias=st["nm"][:, c:c + 1],
                          scale=st["r"][:, c:c + 1], deps=dps))
    return toks


def emit_ln_stats(p, x_ap, nch, W, st, cols, deps):
    tacc = None
    for c in range(nch):
        p.add("act", "activation", st["sq"][:], x_ap[:, c, :], AF.Copy,
              accum_out=st["s1"][:, cols][:, c:c + 1], deps=deps)
        tacc = p.add("act", "activation", st["sq"][:], x_ap[:, c, :], AF.Square,
                     accum_out=st["s2"][:, cols][:, c:c + 1])
    s1, s2, m, v, r, nm, tmp = (st["s1"][:, cols], st["s2"][:, cols],
                                st["m"][:, cols], st["v"][:, cols],
                                st["r"][:, cols], st["nm"][:, cols],
                                st["tmp"][:, cols])
    p.add("dve", "tensor_scalar", m, s1, 1.0 / W, None, ALU.mult, deps=[tacc])
    p.add("dve", "tensor_scalar", v, s2, 1.0 / W, None, ALU.mult)
    p.add("dve", "tensor_tensor", tmp, m, m, ALU.mult)
    p.add("dve", "tensor_sub", v, v, tmp)
    p.add("dve", "tensor_scalar", v, v, LN_EPS, None, ALU.add)
    p.add("dve", "reciprocal", r, v)
    for _ in range(4):
        p.add("dve", "tensor_tensor", tmp, r, r, ALU.mult)
        p.add("dve", "tensor_tensor", tmp, tmp, v, ALU.mult)
        p.add("dve", "tensor_scalar", tmp, tmp, -0.5, 1.5, ALU.mult, ALU.add)
        p.add("dve", "tensor_tensor", r, r, tmp, ALU.mult)
    p.add("dve", "tensor_tensor", nm, m, r, ALU.mult)
    return p.add("dve", "tensor_scalar", nm, nm, -1.0, None, ALU.mult)


def make_qstate(ctx, nc, sbfn, prefix, QW, nsets):
    NB = QW // 16
    sets = []
    for r in range(nsets):
        sets.append({
            "am": sbfn(ctx, f"{prefix}am{r}", [128, NB], F32),
            "scale": sbfn(ctx, f"{prefix}sc{r}", [128, NB], F32),
            "recip": sbfn(ctx, f"{prefix}rc{r}", [128, NB], F32),
            "sclb": sbfn(ctx, f"{prefix}sb{r}", [128, NB], BF16),
            "y": sbfn(ctx, f"{prefix}y{r}", [128, QW], F32),
            "w1": sbfn(ctx, f"{prefix}w1{r}", [128, QW], F32),
            "A": sbfn(ctx, f"{prefix}A{r}", [128, QW], BF16),
            "B": sbfn(ctx, f"{prefix}B{r}", [128, QW], BF16),
            "sg": sbfn(ctx, f"{prefix}sg{r}", [128, QW], BF16),
        })
    return {"rot": 0, "sets": sets, "war": [dict() for _ in range(nsets)]}


def emit_quant(p, src_ap, out_ap, F, q, deps, out_war=None):
    r = q["rot"] % len(q["sets"])
    q["rot"] += 1
    b = q["sets"][r]
    war = q["war"][r]
    nb = F // 16
    src3 = src_ap.rearrange("p (nb s) -> p nb s", s=16)
    tred = p.add("dve", "tensor_reduce", b["am"][:, 0:nb], src3, AX.X, ALU.max,
                 apply_absolute_value=True, deps=deps + [war.get("am", [])])
    tsc = p.add("dve", "tensor_scalar", b["scale"][:, 0:nb], b["am"][:, 0:nb],
                1.0 / 6.0, 1e-12, ALU.mult, ALU.max, deps=[tred])
    tsb = p.add("dve", "tensor_copy", b["sclb"][:, 0:nb], b["scale"][:, 0:nb],
                deps=[war.get("sclb", [])])
    trc = p.add("dve", "reciprocal", b["recip"][:, 0:nb], b["scale"][:, 0:nb],
                deps=[war.get("recip", [])])
    tabs = p.add("act", "activation", b["y"][:, 0:F], src_ap, AF.Abs,
                 deps=deps + [war.get("y", [])])
    tsg = p.add("act", "activation", b["sg"][:, 0:F], src_ap, AF.Sign,
                deps=deps + [war.get("sg", [])])
    y3 = b["y"][:, 0:F].rearrange("p (nb s) -> p nb s", s=16)
    rb = b["recip"][:, 0:nb].rearrange("p nb -> p nb ()").broadcast_to([128, nb, 16])
    ty = p.add("dve", "tensor_tensor", y3, y3, rb, ALU.mult, deps=[tabs, trc])
    yF = b["y"][:, 0:F]
    wF = b["w1"][:, 0:F]
    aF = b["A"][:, 0:F]
    bF = b["B"][:, 0:F]
    p.add("dve", "tensor_scalar", wF, yF, 2.0, C_05, ALU.min, ALU.add,
          deps=[war.get("w1", [])])
    p.add("dve", "tensor_scalar", aF, wF, C_05, None, ALU.subtract,
          deps=[war.get("A", [])])
    p.add("dve", "tensor_scalar", wF, yF, 2.0, 4.0, ALU.max, ALU.min)
    tB = p.add("dve", "tensor_scalar", bF, wF, C_INT, C_INT + 2.0, ALU.add,
               ALU.subtract, deps=[war.get("B", [])])
    p.add("dve", "tensor_tensor", aF, aF, bF, ALU.add)
    tG = p.add("dve", "tensor_scalar", bF, yF, 5.0, 2.0, ALU.is_gt, ALU.mult)
    tlast = p.add("dve", "tensor_tensor", aF, aF, bF, ALU.add)
    tsgm = p.add("dve", "tensor_tensor", aF, aF, b["sg"][:, 0:F], ALU.mult,
                 deps=[tsg])
    a3 = b["A"][:, 0:F].rearrange("p (nb s) -> p nb s", s=16)
    o3 = out_ap.rearrange("p (nb s) -> p nb s", s=16)
    scb3 = b["sclb"][:, 0:nb].rearrange("p nb -> p nb ()").broadcast_to([128, nb, 16])
    tout = p.add("dve", "tensor_tensor", o3, a3, scb3, ALU.mult,
                 deps=[tsb, out_war])
    war["am"] = [tsc]
    war["sclb"] = [tout]
    war["recip"] = [ty]
    war["y"] = [tG]
    war["w1"] = [tB]
    war["sg"] = [tsgm]
    war["A"] = [tout]
    war["B"] = [tlast]
    return tout, [tred, tabs, tsg]



def build_program(S=8192, E=1024, I=2816):
    EC = E // 128
    NCH = S // 128
    NSLOT = NCH // 2
    NT = S // 256
    G = min(8, NSLOT)
    IC = I // 128
    IH = I // 2
    assert NSLOT % G == 0 and E % 128 == 0 and I % 256 == 0 and S % 1024 == 0
    NG = NSLOT // G
    HALFD = IC // 2
    GUH = (IC + 1) // 2
    iblocks = []
    off = 0
    while off < I:
        w = min(512, I - off)
        iblocks.append((off, w))
        off += w
    NIB = len(iblocks)
    NBMAX = max(E, IH, 512) // 16
    QW = max(E, IH, 512)

    nc = bass.Bass("TRN2", target_bir_lowering=False)

    xp = nc.dram_tensor("xp", [S, E], F32, kind="ExternalInput").ap()
    wqtd = nc.dram_tensor("wqt", [E, E], F32, kind="ExternalInput").ap()
    wktd = nc.dram_tensor("wkt", [E, E], F32, kind="ExternalInput").ap()
    wvtd = nc.dram_tensor("wvt", [E, E], F32, kind="ExternalInput").ap()
    bqd = nc.dram_tensor("bq", [128, EC], F32, kind="ExternalInput").ap()
    bkd = nc.dram_tensor("bk", [128, EC], F32, kind="ExternalInput").ap()
    bvd = nc.dram_tensor("bv", [1, E], F32, kind="ExternalInput").ap()
    wgd = nc.dram_tensor("wg", [I, E], F32, kind="ExternalInput").ap()
    wud = nc.dram_tensor("wu", [I, E], F32, kind="ExternalInput").ap()
    wdd = nc.dram_tensor("wd", [E, I], F32, kind="ExternalInput").ap()
    mskd = nc.dram_tensor("msk", [2, 128, 128], F32, kind="ExternalInput").ap()
    yd = nc.dram_tensor("y", [NSLOT, 128, E], F32, kind="ExternalOutput").ap()
    kfm = nc.dram_tensor("kfm", [EC, 128, S], BF16, kind="Internal").ap()
    vtm = nc.dram_tensor("vtm", [NCH, 128, E], BF16, kind="Internal").ap()
    qfm = nc.dram_tensor("qfm", [EC, 128, 128 * NSLOT], BF16, kind="Internal").ap()
    x1d = nc.dram_tensor("x1d", [NSLOT, 128, E], F32, kind="Internal").ap()
    wgtd = nc.dram_tensor("wgtd", [EC, 128, I], BF16, kind="Internal").ap()
    wutd = nc.dram_tensor("wutd", [EC, 128, I], BF16, kind="Internal").ap()
    wdtd = nc.dram_tensor("wdtd", [IC, 128, E], BF16, kind="Internal").ap()

    outer = contextlib.ExitStack()
    with outer:
        sems = {}
        for name in ("pe", "dve", "act", "pool", "d_init",
                     "xa0", "xa1", "xa2", "wl0", "wl1", "kst0", "kst1", "kk2",
                     "vst0", "vst1", "qst", "wfl",
                     "kk0", "kk1", "vv0", "vv1", "qgl", "xrl", "xs0", "xs1",
                     "wcl", "xl0", "xl1", "xl2", "ys0", "ys1"):
            sems[name] = outer.enter_context(nc.semaphore(name))
        state = Prog.fresh_state(sems)

        def sb(ctx, name, shape, dt):
            return ctx.enter_context(nc.sbuf_tensor(name, shape, dt))

        def psr(ctx, name, shape, dt):
            return ctx.enter_context(nc.psum_tensor(name, shape, dt))

        iden = sb(outer, "iden", [128, 128], BF16)
        cneg = sb(outer, "cneg", [128, 1], F32)
        ones_l = sb(outer, "ones_l", [128, 1], BF16)
        msk = sb(outer, "msk_sb", [128, 2, 128], F32)
        bq128 = sb(outer, "bq128", [128, EC], F32)
        bk128 = sb(outer, "bk128", [128, EC], F32)
        bq32 = sb(outer, "bq32", [128, EC], F32)

        with contextlib.ExitStack() as actx:
            p = Prog(nc, sems, state)
            xt = [sb(actx, f"xt{i}", [128, 2, E], F32) for i in range(3)]
            xln = [sb(actx, f"xln{i}", [128, 2, E], BF16) for i in range(2)]
            xfm = [sb(actx, f"xfm{i}", [128, EC, 256], BF16) for i in range(2)]
            ktb = [sb(actx, f"ktb{i}", [128, EC, 256], BF16) for i in range(2)]
            vtb = [sb(actx, f"vtb{i}", [128, 2, E], BF16) for i in range(2)]
            qtb = sb(actx, "qtb", [128, EC, 512], BF16)
            wqkv = {nm: sb(actx, nm + "t_sb", [128, EC, E], BF16)
                    for nm in ("wq", "wk", "wv")}
            wld = [sb(actx, f"wld{i}", [128, E], F32) for i in range(2)]
            st = {k: sb(actx, "ln_" + k, [128, 6], F32)
                  for k in ("s1", "s2", "m", "v", "r", "nm", "tmp")}
            st["sq"] = sb(actx, "ln_sq", [128, E], BF16)
            bvf = sb(actx, "bvf", [128, E], F32)
            sbv = sb(actx, "sbv", [1, E], F32)
            sbvb = sb(actx, "sbvb", [1, E], BF16)
            ones1 = sb(actx, "ones1", [1, 128], BF16)

            ptr = [psr(actx, f"ptr{i}", [128, 1024], BF16) for i in range(2)]
            kps = [psr(actx, f"kps{i}", [128, 512], F32) for i in range(2)]
            qps = [psr(actx, f"qps{i}", [128, 512], F32) for i in range(2)]
            vps = [psr(actx, f"vps{i}", [128, 512], F32) for i in range(2)]

            warb = {"ptr": {0: None, 1: None}, "kps": {0: None, 1: None},
                    "qps": {0: None, 1: None}, "vps": {0: None, 1: None}}
            p.add("pool", "memset", iden[:], 0.0)
            tid = p.add("pool", "affine_select", out=iden[:], in_=iden[:],
                        compare_op=ALU.not_equal, fill=1.0, base=0,
                        pattern=[[-1, 128]], channel_multiplier=1)
            p.add("pool", "memset", cneg[:], EXP_SHIFT)
            p.add("pool", "memset", ones_l[:], 1.0)
            t_ones = p.add("pool", "memset", ones1[:], 1.0)
            p.add("sp", "dma_start", msk[:], mskd.rearrange("i k q -> k i q"),
                  dsem="d_init")
            p.add("sp", "dma_start", bq128[:], bqd, dsem="d_init")
            p.add("sp", "dma_start", bk128[:], bkd, dsem="d_init")
            t_init = p.add("sp", "dma_start", sbv[:], bvd, dsem="d_init")
            t_msk = t_bq = t_bk = t_bv = t_init
            p.add("dve", "tensor_scalar", bq32[:], bq128[:], 1.0 / 32.0, None,
                  ALU.mult, deps=[t_bq])
            tbvb = p.add("dve", "tensor_copy", sbvb[:], sbv[:], deps=[t_bv])
            tbvf = None
            for h in range(0, E, 512):
                w = min(512, E - h)
                pi = (h // 512) % 2
                tmm = p.add("pe", "matmul", vps[pi][:, 0:w], ones1[:],
                            sbvb[:, h:h + w], start=True, stop=True,
                            deps=[tbvb, t_ones, warb["vps"][pi]])
                tbvf = p.add("dve", "tensor_copy", bvf[:, h:h + w],
                             vps[pi][:, 0:w], deps=[tmm])
                warb["vps"][pi] = tbvf

            wcast = {}
            widx = 0
            for nm, dsrc in (("wq", wqtd), ("wk", wktd), ("wv", wvtd)):
                for ec in range(EC):
                    buf = wld[widx % 2]
                    war = wcast.get(widx - 2)
                    tl = p.add("sp", "dma_start", buf[:, 0:E],
                               dsrc.rearrange("(ec pp) f -> ec pp f", pp=128)[ec],
                               deps=[war] if war else [], dsem=f"wl{widx % 2}")
                    wcast[widx] = p.add("dve", "tensor_copy", wqkv[nm][:, ec, :],
                                        buf[:, 0:E], deps=[tl])
                    widx += 1
            t_wq = wcast[EC - 1]
            t_wk = wcast[2 * EC - 1]
            t_wv = wcast[3 * EC - 1]

            xpt = xp.rearrange("(t c pp) e -> t pp c e", c=2, pp=128)
            tt = {}

            def a_load(t):
                if t >= NT:
                    return
                tt[("ld", t)] = p.add("sp", "dma_start", xt[t % 3][:], xpt[t],
                                      deps=[tt.get(("apply", t - 3))],
                                      dsem=f"xa{t % 3}")

            def a_stats(t):
                if t >= NT:
                    return
                cols = slice((t % 3) * 2, (t % 3) * 2 + 2)
                tt[("st", t)] = emit_ln_stats(p, xt[t % 3][:], 2, E, st, cols,
                                              [tt[("ld", t)],
                                               tt.get(("apply", t - 3))])

            def a_apply(t):
                if t >= NT:
                    return
                toks = []
                for c in range(2):
                    col = (t % 3) * 2 + c
                    dps = [tt[("st", t)]]
                    if c == 0:
                        dps.append(tt.get(("trdone", t - 2)))
                    toks.append(p.add("act", "activation", xln[t % 2][:, c, :],
                                      xt[t % 3][:, c, :], AF.Identity,
                                      bias=st["nm"][:, col:col + 1],
                                      scale=st["r"][:, col:col + 1], deps=dps))
                tt[("applies", t)] = toks
                tt[("apply", t)] = toks[-1]

            a_load(0); a_stats(0)
            a_load(1); a_stats(1)
            a_apply(0)

            for t in range(NT):
                bi = t % 2
                a_load(t + 2)
                apply_toks = tt[("applies", t)]
                dcp = None
                for c in range(2):
                    tts = None
                    for ec in range(EC):
                        dps = ()
                        if ec == 0:
                            dps = [apply_toks[c], tid, warb["ptr"][c]]
                        tts = p.add("pe", "transpose",
                                    ptr[c][:, ec * 128:(ec + 1) * 128],
                                    xln[bi][:, c, ec * 128:(ec + 1) * 128],
                                    iden[:], deps=dps, inc=(ec == EC - 1))
                    dcp = p.add("dve", "tensor_copy",
                                xfm[bi][:, :, c * 128:(c + 1) * 128],
                                ptr[c][:, 0:EC * 128].rearrange(
                                    "p (ec qq) -> p ec qq", qq=128),
                                deps=[tts, tt.get(("projdone", t - 2))])
                    warb["ptr"][c] = dcp
                tt[("trdone", t)] = dcp
                txfm = dcp
                kev = None
                for fc in range(EC):
                    tmm = None
                    for ec in range(EC):
                        dps = ()
                        if ec == 0:
                            dps = [txfm, t_wk, warb["kps"][fc % 2]]
                        tmm = p.add("pe", "matmul", kps[fc % 2][:, 0:256],
                                    wqkv["wk"][:, ec, fc * 128:(fc + 1) * 128],
                                    xfm[bi][:, ec, :],
                                    start=(ec == 0), stop=(ec == EC - 1),
                                    deps=dps, inc=(ec == EC - 1))
                    kev = p.add("act", "activation", ktb[bi][:, fc, :],
                                kps[fc % 2][:, 0:256], AF.Identity,
                                bias=bk128[:, fc:fc + 1],
                                deps=[tmm, t_bk, tt.get(("kst", t - 2))])
                    warb["kps"][fc % 2] = kev
                tt[("kst", t)] = p.add(
                    "pool", "dma_start",
                    kfm.rearrange("ec pp k -> pp ec k")[:, :, 256 * t:256 * (t + 1)],
                    ktb[bi][:], deps=[kev], dsem=f"kst{bi}")
                qev = None
                for fc in range(EC):
                    tmm = None
                    for ec in range(EC):
                        dps = ()
                        if ec == 0:
                            dps = [txfm, t_wq, warb["qps"][fc % 2]]
                        tmm = p.add("pe", "matmul", qps[fc % 2][:, 0:128],
                                    wqkv["wq"][:, ec, fc * 128:(fc + 1) * 128],
                                    xfm[bi][:, ec, 0:128],
                                    start=(ec == 0), stop=(ec == EC - 1),
                                    deps=dps, inc=(ec == EC - 1))
                    qev = p.add("act", "activation",
                                qtb[:, fc, (t % 4) * 128:(t % 4 + 1) * 128],
                                qps[fc % 2][:, 0:128], AF.Identity,
                                bias=bq32[:, fc:fc + 1], scale=1.0 / 32.0,
                                deps=[tmm, tt.get(("qst", 4 * (t // 4) - 1))])
                    warb["qps"][fc % 2] = qev
                if t % 4 == 3 or t == NT - 1:
                    u = t // 4
                    ncols = (t % 4 + 1) * 128
                    tt[("qst", t)] = p.add(
                        "pool", "dma_start",
                        qfm.rearrange("ec pp k -> pp ec k")[:, :, 512 * u:512 * u + ncols],
                        qtb[:, :, 0:ncols], deps=[qev], dsem="qst")
                vev = None
                vpi = 0
                for c in range(2):
                    for h in range(0, E, 512):
                        w = min(512, E - h)
                        pi = vpi % 2
                        vpi += 1
                        tmm = None
                        for ec in range(EC):
                            dps = ()
                            if ec == 0:
                                dps = [txfm, t_wv, tbvf, warb["vps"][pi]]
                            tmm = p.add("pe", "matmul", vps[pi][:, 0:w],
                                        xfm[bi][:, ec, c * 128:(c + 1) * 128],
                                        wqkv["wv"][:, ec, h:h + w],
                                        start=(ec == 0), stop=(ec == EC - 1),
                                        deps=dps, inc=(ec == EC - 1))
                        vev = p.add("dve", "tensor_tensor", vtb[bi][:, c, h:h + w],
                                    vps[pi][:, 0:w], bvf[:, h:h + w], ALU.add,
                                    deps=[tmm, tbvf, tt.get(("vst", t - 2))])
                        warb["vps"][pi] = vev
                tt[("vst", t)] = p.add(
                    "pool", "dma_start",
                    vtm.rearrange("kc pp e -> pp kc e")[:, 2 * t:2 * t + 2, :],
                    vtb[bi][:], deps=[vev], dsem=f"vst{bi}")
                tt[("projdone", t)] = vev
                tt[("lastuse", t)] = vev
                a_stats(t + 2)
                a_apply(t + 1)

            blk = actx.enter_context(nc.Block())
            p.emit(blk)

        with contextlib.ExitStack() as bctx:
            p = Prog(nc, sems, state)
            p.barrier()
            qg = sb(bctx, "qg", [128, EC, 128 * G], BF16)
            kk = [sb(bctx, f"kk{i}", [128, EC, 512], BF16) for i in range(3)]
            vv = [sb(bctx, f"vv{i}", [128, 4, E], BF16) for i in range(2)]
            og = sb(bctx, "og", [128, G, E + 1], F32)
            pt_all = [sb(bctx, f"pta{i}", [128, 4, 512], BF16) for i in range(2)]
            xr = sb(bctx, "xr", [128, E], F32)
            x1t = [sb(bctx, f"x1t{i}", [128, E], F32) for i in range(2)]
            lsb = sb(bctx, "lsb", [128, G], F32)
            linv = sb(bctx, "linv", [128, G], F32)
            wldb = [sb(bctx, f"wldb{i}", [128, max(E, IH)], F32)
                    for i in range(2)]
            wqbb = [sb(bctx, f"wqbb{i}", [128, max(E, IH)], BF16)
                    for i in range(2)]
            wstage = sb(bctx, "wstage", [128, E * IC // 2], BF16)
            qa = make_qstate(bctx, nc, sb, "qa_", QW, 2)
            stq = [psr(bctx, f"stq{i}", [128, 512], F32) for i in range(2)]
            avp = [psr(bctx, f"avp{i}", [128, E], F32) for i in range(2)]
            lps = psr(bctx, "lps", [128, 512], F32)
            wptr = psr(bctx, "wptr", [128, 1024], BF16)

            bt = {}
            sgi = 0
            bct = 0
            sit = 0
            itlast = {}
            pend_prev = []

            def emit_tail(tc):
                s_it, ar, kbi, nkc, kt, jj, g_, pb, boff, texp, tvl = (
                    tc["sit"], tc["ar"], tc["kbi"], tc["nkc"], tc["kt"],
                    tc["jj"], tc["g"], tc["pb"], tc["boff"], tc["texp"], tc["tvl"])
                tav = None
                for h in range(0, E, 512):
                    w = min(512, E - h)
                    for kc in range(nkc):
                        dps = ()
                        if kc == 0 and h == 0:
                            dps = [texp, tvl, bt.get(("og", s_it - 2))]
                        tav = p.add("pe", "matmul", avp[ar][:, h:h + w],
                                    pt_all[pb][:, kc, boff:boff + 128],
                                    vv[kbi][:, kc, h:h + w],
                                    start=(kc == 0), stop=(kc == nkc - 1),
                                    skip_group_check=True,
                                    deps=dps, inc=False)
                lcol = s_it
                assert lcol < 512
                tl_ = None
                for kc in range(nkc):
                    tl_ = p.add("pe", "matmul", lps[:, lcol:lcol + 1],
                                pt_all[pb][:, kc, boff:boff + 128],
                                ones_l[:],
                                start=(kc == 0), stop=(kc == nkc - 1),
                                skip_group_check=True,
                                inc=(kc == nkc - 1))
                bt[("l", s_it)] = tl_
                ogkey = ("ogslot", g_, jj)
                if kt == 0:
                    tog = p.add("dve", "tensor_copy", og[:, jj, 0:E],
                                avp[ar][:, 0:E],
                                deps=[tl_, bt.get(("fin", g_ - 1, jj))])
                    tog = p.add("dve", "tensor_copy", og[:, jj, E:E + 1],
                                lps[:, lcol:lcol + 1])
                else:
                    tog = p.add("dve", "tensor_add", og[:, jj, 0:E],
                                og[:, jj, 0:E], avp[ar][:, 0:E],
                                deps=[tl_, bt.get(ogkey)])
                    tog = p.add("dve", "tensor_add", og[:, jj, E:E + 1],
                                og[:, jj, E:E + 1], lps[:, lcol:lcol + 1])
                bt[ogkey] = tog
                bt[("og", s_it)] = tog

            wjobs = ([("wg", None, ti) for ti in range(IC)] +
                     [("wu", None, ti) for ti in range(IC)] +
                     [("wd", h, ti) for h in range(2) for ti in range(EC)])
            nwj = len(wjobs)
            wj = {"qtok": {}, "srel": {}, "wqb_war": {}, "stage_war": None,
                  "ptr_war": None}

            def wjob_head(idx):
                nm, h, ti = wjobs[idx]
                bi = idx % 2
                if nm == "wd":
                    F_ = IH
                    src_ = wdd.rearrange("(t pp) i -> t pp i", pp=128)[ti][:, IH * h:IH * (h + 1)]
                else:
                    F_ = E
                    dsrc = wgd if nm == "wg" else wud
                    src_ = dsrc.rearrange("(t pp) f -> t pp f", pp=128)[ti]
                tl = p.add("act", "dma_start", wldb[bi][:, 0:F_], src_,
                           deps=[wj["srel"].get(idx - 2, [])], dsem=f"wl{bi}")
                tq, srel = emit_quant(p, wldb[bi][:, 0:F_], wqbb[bi][:, 0:F_],
                                      F_, qa, [tl],
                                      out_war=wj["wqb_war"].get(bi))
                wj["qtok"][idx] = tq
                wj["srel"][idx] = srel

            def wjob_tail(idx):
                nm, h, ti = wjobs[idx]
                bi = idx % 2
                F_ = IH if nm == "wd" else E
                tq = wj["qtok"][idx]
                nblk = F_ // 128
                base = 0
                last_cp = None
                while base < nblk:
                    cnt = min(8, nblk - base)
                    tts = None
                    for b_ in range(cnt):
                        tts = p.add("pe", "transpose",
                                    wptr[:, b_ * 128:(b_ + 1) * 128],
                                    wqbb[bi][:, (base + b_) * 128:(base + b_ + 1) * 128],
                                    iden[:],
                                    deps=([tq, wj["ptr_war"]]
                                          if b_ == 0 else ()),
                                    inc=(b_ == cnt - 1))
                    srcv = wptr[:, 0:cnt * 128].rearrange(
                        "p (b_ qq) -> p b_ qq", qq=128)
                    if nm == "wd":
                        dst = wstage[:].rearrange("p (ic e) -> p ic e", e=E)[
                            :, base:base + cnt, ti * 128:(ti + 1) * 128]
                    else:
                        loc = ti % GUH
                        dst = wstage[:].rearrange("p (ec i) -> p ec i", ec=EC)[
                            :, :, loc * 128:(loc + 1) * 128]
                    stage_deps = [tts]
                    if wj["stage_war"] is not None:
                        stage_deps.append(wj["stage_war"])
                    last_cp = p.add("dve", "tensor_copy", dst, srcv,
                                    deps=stage_deps)
                    wj["ptr_war"] = last_cp
                    base += cnt
                wj["wqb_war"][bi] = last_cp
                if nm == "wd":
                    if ti == EC - 1:
                        dst = wdtd.rearrange("ic pp e -> pp ic e")[
                            :, h * HALFD:(h + 1) * HALFD, :]
                        srcd = wstage[:].rearrange("p (ic e) -> p ic e", e=E)[
                            :, 0:HALFD, :]
                        wj["stage_war"] = p.add("pool", "dma_start", dst, srcd,
                                                deps=[last_cp], dsem="wfl")
                else:
                    loc = ti % GUH
                    if loc == GUH - 1 or ti == IC - 1:
                        hidx = ti // GUH
                        base_i = hidx * GUH * 128
                        ncols = (loc + 1) * 128
                        ddst = wgtd if nm == "wg" else wutd
                        dst = ddst.rearrange("ec pp i -> pp ec i")[
                            :, :, base_i:base_i + ncols]
                        srcd = wstage[:].rearrange("p (ec i) -> p ec i", ec=EC)[
                            :, :, 0:ncols]
                        wj["stage_war"] = p.add("pool", "dma_start", dst, srcd,
                                                deps=[last_cp], dsem="wfl")

            NKT_TOT = sum((256 * (gg * G + G) + 511) // 512 for gg in range(NG))
            wji = 0
            gkt = 0

            for g in range(NG):
                s0 = g * G
                tqg = p.add("sp", "dma_start", qg[:],
                            qfm.rearrange("ec pp k -> pp ec k")[:, :, 128 * s0:128 * (s0 + G)],
                            deps=[bt.get(("qcons", g - 1))], dsem="qgl")
                nkt = (256 * (s0 + G) + 511) // 512
                for kt in range(nkt):
                    kbi = kt % 2
                    kki = kt % 3
                    prev2 = itlast.get((g, kt - 2), itlast.get((g - 1, "last")))
                    kwar = bt.get(("l", prev2)) if prev2 is not None else None
                    prev3 = itlast.get((g, kt - 3), itlast.get((g - 1, "last")))
                    kwar3 = bt.get(("l", prev3)) if prev3 is not None else None
                    tkl = p.add("sp", "dma_start", kk[kki][:],
                                kfm.rearrange("ec pp k -> pp ec k")[:, :, 512 * kt:512 * (kt + 1)],
                                deps=[kwar3], dsem=f"kk{kki}")
                    tvl = p.add("sp", "dma_start", vv[kbi][:],
                                vtm.rearrange("kc pp e -> pp kc e")[:, 4 * kt:4 * kt + 4, :],
                                deps=[kwar], dsem=f"vv{kbi}")
                    full, single = [], []
                    for jj in range(G):
                        keys = 256 * (s0 + jj + 1) - 512 * kt
                        if keys <= 0:
                            continue
                        (full if keys > 512 else single).append(jj)
                    batches = [full[i:i + 4] for i in range(0, len(full), 4)]
                    batches += [[jj] for jj in single]
                    for bslots in batches:
                        jj0, w = bslots[0], len(bslots)
                        assert bslots == list(range(jj0, jj0 + w))
                        keys0 = 256 * (s0 + bslots[-1] + 1) - 512 * kt
                        nkc = min(4, max(k_ // 128 for k_ in
                                         [256 * (s0 + j_ + 1) - 512 * kt
                                          for j_ in bslots]))
                        is_last = keys0 <= 512
                        pb = bct % 2
                        bct += 1
                        texp = None
                        for kc in range(nkc):
                            srot = sgi % 2
                            sgi += 1
                            tmm = None
                            for ec in range(EC):
                                dps = ()
                                if ec == 0:
                                    dps = [tkl, tqg, bt.get(("sexp", sgi - 1 - 2))]
                                tmm = p.add("pe", "matmul",
                                            stq[srot][:, 0:w * 128],
                                            kk[kki][:, ec, kc * 128:(kc + 1) * 128],
                                            qg[:, ec, (jj0) * 128:(jj0 + w) * 128],
                                            start=(ec == 0), stop=(ec == EC - 1),
                                            deps=dps,
                                            inc=(ec == EC - 1))
                            bt[("sc", sgi - 1)] = tmm
                            tsc = tmm
                            if is_last and w == 1 and kc >= nkc - 2:
                                mi = kc - (nkc - 2)
                                tsc = p.add("dve", "tensor_tensor",
                                            stq[srot][:, 0:128],
                                            stq[srot][:, 0:128],
                                            msk[:, mi, :], ALU.add,
                                            deps=[tsc, t_msk])
                            texp = p.add("act", "activation",
                                         pt_all[pb][:, kc, 0:w * 128],
                                         stq[srot][:, 0:w * 128], AF.Exp,
                                         bias=cneg[:],
                                         deps=[tsc, bt.get(("btail", bct - 1 - 2))])
                            bt[("sexp", sgi - 1)] = texp
                        newtails = []
                        for bi_, jj in enumerate(bslots):
                            keys = 256 * (s0 + jj + 1) - 512 * kt
                            newtails.append(dict(
                                sit=sit, ar=sit % 2, kbi=kbi,
                                nkc=min(4, keys // 128), kt=kt, jj=jj, g=g,
                                pb=pb, boff=bi_ * 128, texp=texp, tvl=tvl))
                            sit += 1
                        for tc in pend_prev:
                            emit_tail(tc)
                        if pend_prev:
                            bt[("btail", pend_prev[0]["bct"])] = bt[("l", pend_prev[-1]["sit"])]
                        for tc in newtails:
                            tc["bct"] = bct - 1
                        pend_prev = newtails
                        nb_ = len(batches)
                        bix_ = batches.index(bslots) + 1
                        tgt = (gkt * nb_ + bix_) * nwj // (NKT_TOT * nb_)
                        while wji < nwj and wji <= tgt - 1:
                            if wji >= 2:
                                wjob_tail(wji - 2)
                            wjob_head(wji)
                            wji += 1
                    itlast[(g, kt)] = sit - 1
                    gkt += 1
                itlast[(g, "last")] = sit - 1
                for tc in pend_prev:
                    emit_tail(tc)
                if pend_prev:
                    bt[("btail", pend_prev[0]["bct"])] = bt[("l", pend_prev[-1]["sit"])]
                pend_prev = []
                bt[("qcons", g)] = bt[("sc", sgi - 1)]
                ogdone = [bt.get(("ogslot", g, jj)) for jj in range(G)]
                tlev = p.add("dve", "tensor_copy",
                             lsb[:].rearrange("p g -> p g ()"),
                             og[:, :, E:E + 1],
                             deps=[bt[("og", sit - 1)]] + ogdone)
                p.add("dve", "reciprocal", linv[:], lsb[:])
                for jj in range(G):
                    j = s0 + jj
                    txr = p.add("act", "dma_start", xr[:],
                                xp.rearrange("(n pp) e -> n pp e", pp=128)[2 * j],
                                deps=[bt.get("lastfin")], dsem="xrl")
                    p.add("dve", "tensor_scalar", x1t[jj % 2][:],
                          og[:, jj, 0:E], linv[:, jj:jj + 1], None, ALU.mult,
                          deps=[bt.get(("x1war", jj % 2)),
                                bt.get(("ogslot", g, jj))])
                    tfin = p.add("dve", "tensor_add", x1t[jj % 2][:],
                                 x1t[jj % 2][:], xr[:], deps=[txr])
                    bt[("fin", g, jj)] = tfin
                    bt["lastfin"] = tfin
                    bt[("x1war", jj % 2)] = p.add(
                        "pool", "dma_start", x1d[j], x1t[jj % 2][:],
                        deps=[tfin], dsem=f"xs{jj % 2}")

            while wji < nwj:
                if wji >= 2:
                    wjob_tail(wji - 2)
                wjob_head(wji)
                wji += 1
            for j_ in range(max(0, nwj - 2), nwj):
                wjob_tail(j_)

            blk = bctx.enter_context(nc.Block())
            p.emit(blk)

        with contextlib.ExitStack() as cctx:
            p = Prog(nc, sems, state)
            p.barrier()
            wgt = sb(cctx, "wgt_sb", [128, EC, I], BF16)
            wut = sb(cctx, "wut_sb", [128, EC, I], BF16)
            wdt = sb(cctx, "wdt_sb", [128, IC, E], BF16)
            x1b = [sb(cctx, f"x1b{i}", [128, E], F32) for i in range(2)]
            xlnc = sb(cctx, "xlnc", [128, E], F32)
            xqb = sb(cctx, "xqb", [128, E], BF16)
            xqf = [sb(cctx, f"xqf{i}", [128, EC, 128], BF16) for i in range(2)]
            sil = [sb(cctx, f"sil{i}", [128, 512], F32) for i in range(2)]
            hbuf = [sb(cctx, f"hbuf{i}", [128, 1024], F32) for i in range(2)]
            hqb = sb(cctx, "hqb", [128, 1024], BF16)
            hqf = [sb(cctx, f"hqf{i}", [128, 8, 128], BF16) for i in range(2)]
            ytb = sb(cctx, "ytb", [128, E], F32)
            stc = {k: sb(cctx, "lnc_" + k, [128, 3], F32)
                   for k in ("s1", "s2", "m", "v", "r", "nm", "tmp")}
            stc["sq"] = sb(cctx, "lnc_sq", [128, E], BF16)
            qc = make_qstate(cctx, nc, sb, "qc_", max(E, 512), 2)
            ptc = [psr(cctx, "ptc0", [128, 1024], BF16)]
            gps = [psr(cctx, f"gps{i}", [128, 512], F32) for i in range(3)]
            ups = [psr(cctx, f"ups{i}", [128, 512], F32) for i in range(2)]
            dps_ = psr(cctx, "dps", [128, max(E, 512)], F32)

            p.add("sp", "dma_start", wgt[:],
                  wgtd.rearrange("ec pp i -> pp ec i"), dsem="wcl")
            p.add("act", "dma_start", wut[:],
                  wutd.rearrange("ec pp i -> pp ec i"), dsem="wcl")
            twd = p.add("pool", "dma_start", wdt[:],
                        wdtd.rearrange("ic pp e -> pp ic e"), dsem="wcl")
            twg = twu = twd

            ct = {}
            trc = {"n": 0}

            def c_load(s):
                if s >= NSLOT:
                    return
                ct[("ld", s)] = p.add("sp", "dma_start", x1b[s % 2][:], x1d[s],
                                      deps=[ct.get(("res", s - 2))],
                                      dsem=f"xl{s % 3}")

            def c_stats(s):
                if s >= NSLOT:
                    return
                x1v = x1b[s % 2][:].rearrange("p (c e) -> p c e", c=1)
                ct[("st", s)] = emit_ln_stats(p, x1v, 1, E, stc,
                                              slice(s % 3, s % 3 + 1),
                                              [ct[("ld", s)],
                                               ct.get(("ap", s - 3))])

            def c_stage1(s):
                if s >= NSLOT:
                    return
                bi = s % 2
                tap = p.add("act", "activation", xlnc[:], x1b[s % 2][:],
                            AF.Identity, bias=stc["nm"][:, s % 3:s % 3 + 1],
                            scale=stc["r"][:, s % 3:s % 3 + 1],
                            deps=[ct[("st", s)], ct.get(("xqdone", s - 1))])
                ct[("ap", s)] = tap
                tq, srel = emit_quant(p, xlnc[:], xqb[:], E, qc,
                                      [tap], out_war=ct.get(("xqtr", s - 1)))
                ct[("xq", s)] = tq
                ct[("xqdone", s)] = srel

            def c_stage1_tr(s):
                if s >= NSLOT:
                    return
                bi = s % 2
                ri = 0
                tts = None
                for ec in range(EC):
                    dps = ()
                    if ec == 0:
                        dps = [ct[("xq", s)], tid, ct.get(("ptc", ri))]
                    tts = p.add("pe", "transpose",
                                ptc[ri][:, ec * 128:(ec + 1) * 128],
                                xqb[:, ec * 128:(ec + 1) * 128], iden[:],
                                deps=dps, inc=(ec == EC - 1))
                ct[("xqtr", s)] = tts
                txqf = p.add("act", "activation", xqf[bi][:].rearrange(
                                 "p ec qq -> p (ec qq)"),
                             ptc[ri][:, 0:EC * 128], AF.Copy,
                             deps=[tts, ct.get(("xqfcons", s - 2))])
                ct[("ptc", ri)] = txqf
                ct[("xqf", s)] = txqf

            c_load(0); c_stats(0)
            c_load(1); c_stats(1)
            c_stage1(0); c_stage1_tr(0)

            hgroups = []
            off = 0
            while off < I:
                w = min(1024, I - off)
                hgroups.append((off, w))
                off += w
            NHG = len(hgroups)

            def chunk_head(s, gib, hb, loc, ioff, cw):
                bi = s % 2
                bb = gib % 2
                hbr = hb % 2
                gp, up_ = gps[gib % 3], ups[bb]
                tg = None
                for ec in range(EC):
                    dps = ()
                    if ec == 0:
                        dps = [ct[("xqf", s)], twg, ct.get(("sil", gib - 3))]
                    tg = p.add("pe", "matmul", gp[:, 0:cw],
                               xqf[bi][:, ec, :],
                               wgt[:, ec, ioff:ioff + cw],
                               start=(ec == 0), stop=(ec == EC - 1),
                               deps=dps, inc=(ec == EC - 1))
                ct[("gate_last", s)] = tg
                tu = None
                for ec in range(EC):
                    dps = ()
                    if ec == 0:
                        dps = [twu, ct.get(("h", gib - 2))]
                    tu = p.add("pe", "matmul", up_[:, 0:cw],
                               xqf[bi][:, ec, :],
                               wut[:, ec, ioff:ioff + cw],
                               start=(ec == 0), stop=(ec == EC - 1),
                               deps=dps, inc=(ec == EC - 1))
                ct[("up_last", s)] = tu
                tsil = p.add("act", "activation", sil[bb][:, 0:cw],
                             gp[:, 0:cw], AF.Silu,
                             deps=[tg, ct.get(("h", gib - 2))])
                ct[("sil", gib)] = tsil
                th = p.add("dve", "tensor_tensor", hbuf[hbr][:, loc:loc + cw],
                           sil[bb][:, 0:cw], up_[:, 0:cw], ALU.mult,
                           deps=[tsil, tu, ct.get(("hrel", hb - 2))])
                ct[("h", gib)] = th
                return th

            def group_tail(hb):
                qoff, qw = hgroups[hb % NHG]
                hbr = hb % 2
                nkc = qw // 128
                ri = 0
                tts2 = None
                for kc in range(nkc):
                    dps = ()
                    if kc == 0:
                        dps = [ct[("hq", hb)], ct.get(("ptc", ri))]
                    tts2 = p.add("pe", "transpose",
                                 ptc[ri][:, kc * 128:(kc + 1) * 128],
                                 hqb[:, kc * 128:(kc + 1) * 128],
                                 iden[:], deps=dps, inc=(kc == nkc - 1))
                ct[("hqtr", hb)] = tts2
                thqf = p.add("act", "activation",
                             hqf[hbr][:, 0:nkc, :].rearrange(
                                 "p kc qq -> p (kc qq)"),
                             ptc[ri][:, 0:nkc * 128], AF.Copy,
                             deps=[tts2, ct.get(("hqfcons", hb - 2))])
                ct[("ptc", ri)] = thqf
                s = hb // NHG
                qi = hb % NHG
                td = None
                for kc in range(nkc):
                    icg = qoff // 128 + kc
                    for h in range(0, E, 512):
                        w = min(512, E - h)
                        dps = ()
                        if kc == 0 and h == 0:
                            dps = [thqf, twd]
                            if qi == 0:
                                dps.append(ct.get(("res", s - 1)))
                        td = p.add("pe", "matmul", dps_[:, h:h + w],
                                   hqf[hbr][:, kc, :],
                                   wdt[:, icg, h:h + w],
                                   start=(qi == 0 and kc == 0),
                                   stop=(qi == NHG - 1 and kc == nkc - 1),
                                   skip_group_check=True,
                                   deps=dps,
                                   inc=(kc == nkc - 1 and h + w >= E))
                ct[("hqfcons", hb)] = td
                if qi == NHG - 1:
                    ct[("down", s)] = td

            def residual(s):
                bi = s % 2
                tres = p.add("dve", "tensor_add", ytb[:], x1b[s % 2][:],
                             dps_[:, 0:E],
                             deps=[ct[("down", s)], ct.get(("yst", s - 1))])
                ct[("res", s)] = tres
                ct[("yst", s)] = p.add("pool", "dma_start", yd[s], ytb[:],
                                       deps=[tres], dsem=f"ys{bi}")

            gib = 0
            for s in range(NSLOT):
                for qi, (qoff, qw) in enumerate(hgroups):
                    hb = s * NHG + qi
                    hbr = hb % 2
                    ths = []
                    loc = 0
                    while loc < qw:
                        cw = min(512, qw - loc)
                        ths.append(chunk_head(s, gib, hb, loc, qoff + loc, cw))
                        gib += 1
                        loc += cw
                    if hb >= 1:
                        group_tail(hb - 1)
                    if qi == 2:
                        c_stage1_tr(s + 1)
                    thq, hrel = emit_quant(p, hbuf[hbr][:, 0:qw],
                                           hqb[:, 0:qw], qw, qc, ths,
                                           out_war=ct.get(("hqtr", hb - 1)))
                    ct[("hrel", hb)] = hrel
                    ct[("hq", hb)] = thq
                    if hb >= 1 and (hb - 1) % NHG == NHG - 1:
                        residual((hb - 1) // NHG)
                        c_load((hb - 1) // NHG + 2)
                        c_stats((hb - 1) // NHG + 2)
                    if qi == 1:
                        c_stage1(s + 1)
                ct[("xqfcons", s)] = ct[("up_last", s)]
            group_tail(NSLOT * NHG - 1)
            residual(NSLOT - 1)

            p.wait_only("pool", [Tok("ys0", p.state["count"]["ys0"]),
                                 Tok("ys1", p.state["count"]["ys1"])])
            blk = cctx.enter_context(nc.Block())
            p.emit(blk)

    nc.finalize()
    return nc



def make_core_inputs(x, Wq, bq, Wk, bk, Wv, bv, wg, wu, wd, S, E, I, core):
    n, par = core // 2, core % 2
    xv = np.ascontiguousarray(x[n], dtype=np.float32)
    if par == 1:
        xv = np.ascontiguousarray(
            xv.reshape(S // 256, 2, 128, E)[:, ::-1].reshape(S, E))
    tri = np.where(np.arange(128)[None, :] >= np.arange(128)[:, None],
                   np.float32(0.0), np.float32(NEG_BIG))
    other = np.full((128, 128), (NEG_BIG if par == 0 else 0.0), np.float32)
    msk = np.stack([tri, other]).astype(np.float32)
    return {
        "xp": xv,
        "wqt": np.ascontiguousarray(Wq.T, dtype=np.float32),
        "wkt": np.ascontiguousarray(Wk.T, dtype=np.float32),
        "wvt": np.ascontiguousarray(Wv.T, dtype=np.float32),
        "bq": np.ascontiguousarray(bq.reshape(E // 128, 128).T, dtype=np.float32),
        "bk": np.ascontiguousarray(bk.reshape(E // 128, 128).T, dtype=np.float32),
        "bv": np.ascontiguousarray(bv.reshape(1, E), dtype=np.float32),
        "wg": np.ascontiguousarray(wg, dtype=np.float32),
        "wu": np.ascontiguousarray(wu, dtype=np.float32),
        "wd": np.ascontiguousarray(wd, dtype=np.float32),
        "msk": msk,
    }


def assemble_output(results, S, E, B=4):
    out = np.empty((B, S, E), np.float32)
    for core in range(2 * B):
        n, par = core // 2, core % 2
        yv = np.asarray(results[core]["y"])
        out[n].reshape(S // 256, 2, 128, E)[:, par] = yv
    return out


def kernel(L0_attn_x, L0_attn_Wq, L0_attn_bq, L0_attn_Wk, L0_attn_bk,
           L0_attn_Wv, L0_attn_bv, L0_ffn_w_gate, L0_ffn_w_up, L0_ffn_w_down):
    S, E, I = 8192, 1024, 2816
    x = np.asarray(L0_attn_x, dtype=np.float32)
    Wq, bq, Wk, bk, Wv, bv, wg, wu, wd = [
        np.asarray(a, dtype=np.float32) for a in
        (L0_attn_Wq, L0_attn_bq, L0_attn_Wk, L0_attn_bk, L0_attn_Wv,
         L0_attn_bv, L0_ffn_w_gate, L0_ffn_w_up, L0_ffn_w_down)]
    nc = build_program(S, E, I)
    in_maps = [make_core_inputs(x, Wq, bq, Wk, bk, Wv, bv, wg, wu, wd,
                                S, E, I, core) for core in range(8)]
    res = run_bass_kernel_spmd(nc, in_maps, core_ids=list(range(8)))
    return assemble_output(res.results, S, E)
